# version 06
# baseline (speedup 1.0000x reference)
import numpy as np
import concourse.bass as bass
import concourse.mybir as mybir
from concourse.bass_utils import run_bass_kernel_spmd
from contextlib import ExitStack

F32 = mybir.dt.float32
BF16 = mybir.dt.bfloat16
I32 = mybir.dt.int32
U8 = mybir.dt.uint8
AF = mybir.ActivationFunctionType
OP = mybir.AluOpType
AX = mybir.AxisListType

NCORES = 8
C = 1024
NCI = 8
TOWN = 2048
HALO = 256
TT = TOWN + HALO
NTJ = TT // 128
E = 8
FF = 2048
NFI = FF // 128
CAP = 384
SLOTS = E * CAP
NSJ = SLOTS // 128
LNEPS = 1e-5
BIG = 1.0e9
TAU_ITERS = 11
TAU_W = 8


class B:
    def __init__(self, arena_bytes):
        self.nc = bass.Bass("TRN2", target_bir_lowering=False)
        self.ctx = ExitStack()
        self._arena_bytes = arena_bytes
        self._off = 0
        self._marks = []
        self._sems = {}
        self.cnt = {}

    def start(self):
        self._arena = self.ctx.enter_context(
            self.nc.sbuf_tensor("arena", [128, self._arena_bytes], U8))
        self.block = self.ctx.enter_context(self.nc.Block())

    def alloc(self, cols, dtype):
        nbytes = cols * mybir.dt.size(dtype)
        ap = self._arena.ap()[:, self._off:self._off + nbytes].bitcast(dtype)
        self._off += (nbytes + 63) & ~63
        assert self._off <= self._arena_bytes, f"sbuf arena overflow {self._off}"
        return ap

    def push(self): self._marks.append(self._off)
    def pop(self): self._off = self._marks.pop()

    def sem(self, name):
        if name not in self._sems:
            self._sems[name] = self.ctx.enter_context(self.nc.semaphore(name))
            self.cnt[name] = 0
        return self._sems[name]

    def inc(self, name, instr, k=1):
        instr.then_inc(self.sem(name), k)
        self.cnt[name] += k
        return self.cnt[name]

    def w(self, eng, name, val=None):
        eng.wait_ge(self.sem(name), self.cnt[name] if val is None else val)

    def dinc(self, base, i, instr, per=16):
        return self.inc(base + "AB"[i % 2], instr, per)

    def dwait(self, eng, base, i, per=16):
        eng.wait_ge(self.sem(base + "AB"[i % 2]), per * (i // 2 + 1))

    def dwait_all(self, eng, base, n, per=16):
        if n <= 0:
            return
        eng.wait_ge(self.sem(base + "A"), per * ((n + 1) // 2))
        if n >= 2:
            eng.wait_ge(self.sem(base + "B"), per * (n // 2))

    def V(self, f): self.block.vector(f)
    def S(self, f): self.block.scalar(f)
    def T(self, f): self.block.tensor(f)
    def G(self, f): self.block.gpsimd(f)
    def Q(self, f): self.block.sync(f)



def emit_ln_gate(b, xs, nt, ident_f32, w9s, onescol, onerow, epscol, nhalfcol,
                 psG, psH, psK, out_xn_hbm, out_prob, mrow, murow, rrow,
                 mu_b, r_b, pfx):
    nc = b.nc
    ntj = nt // 128
    F16 = mybir.dt.float16
    sqh = [b.alloc(nt, F16) for _ in range(NCI)]
    ones16 = b.alloc(1, F16)
    raw = b.alloc(ntj * 10, F32)
    mu_t = b.alloc(ntj, F32)
    r_t = b.alloc(ntj, F32)
    scr_t = b.alloc(ntj, F32)
    csrow = b.alloc(16, F32)
    csb = b.alloc(8, F32)
    lscr = b.alloc(ntj * 8, F32)
    smax = b.alloc(ntj, F32)
    trsc = b.alloc(256, F32)
    xnp = [b.alloc(nt, BF16) for _ in range(2)]
    tsub = b.alloc(nt, F32)

    SQ, PEc, VEc, SEc, LD2, PEd, XNS = (pfx + n for n in
                                        ["SQ", "PEc", "VEc", "SEc", "LD2", "PEd", "XNS"])

    def v0(v):
        b.inc(VEc, v.memset(ones16[:, 0:1], 1.0))
    b.V(v0)
    ve_ones = b.cnt[VEc]

    def s1(s):
        for ci in range(NCI):
            b.inc(SQ, s.activation(out=sqh[ci][:], in_=xs[ci][:], func=AF.Square))
    b.S(s1)

    def t1(t):
        b.w(t, SQ, NCI)
        b.w(t, VEc, ve_ones)
        for tj in range(ntj):
            if tj >= 2:
                b.w(t, VEc, ve_ones + tj - 1)
            bkx = [psG, psH][tj % 2]
            qoff = 512 * (tj % 2)
            for ci in range(NCI):
                t.matmul(bkx[:, 0:9], xs[ci][:, 128 * tj:128 * tj + 128],
                         w9s[:, 9 * ci:9 * ci + 9],
                         start=(ci == 0), stop=(ci == NCI - 1), skip_group_check=True)
            for ci in range(NCI):
                ii = t.matmul(psK[:, qoff:qoff + 1],
                              sqh[ci][:, 128 * tj:128 * tj + 128], ones16[:, 0:1],
                              start=(ci == 0), stop=(ci == NCI - 1),
                              skip_group_check=True)
            b.inc(PEc, ii)
    b.T(t1)

    def v_ev(v):
        for tj in range(ntj):
            b.w(v, PEc, tj + 1)
            bkx = [psG, psH][tj % 2]
            qoff = 512 * (tj % 2)
            v.tensor_copy(out=raw[:, 10 * tj:10 * tj + 9], in_=bkx[:, 0:9])
            b.inc(VEc, v.tensor_copy(out=raw[:, 10 * tj + 9:10 * tj + 10],
                                     in_=psK[:, qoff:qoff + 1]))
    b.V(v_ev)
    ve_raw = b.cnt[VEc]

    def t1b(t):
        b.w(t, VEc, ve_raw)
        for ci in range(NCI):
            last = t.matmul(psH[0:1, 0:8], onescol[:, 0:1], w9s[:, 9 * ci:9 * ci + 8],
                            start=(ci == 0), stop=(ci == NCI - 1), skip_group_check=True)
        b.inc(PEc, last)
    b.T(t1b)

    rawv = raw[:, 0:10 * ntj].rearrange("p (tj k) -> p tj k", k=10)
    INV_C = 1.0 / C

    def v1(v):
        b.w(v, PEc, ntj + 1)
        b.w(v, VEc, ve_raw)
        b.inc(VEc, v.tensor_scalar(out=mu_t[:, 0:ntj], in0=rawv[:, :, 8:9].squeeze(2),
                                   scalar1=INV_C, scalar2=None, op0=OP.mult))
        b.inc(VEc, v.tensor_scalar(out=r_t[:, 0:ntj], in0=rawv[:, :, 9:10].squeeze(2),
                                   scalar1=INV_C, scalar2=None, op0=OP.mult))
        b.w(v, VEc)
        b.inc(VEc, v.tensor_mul(out=scr_t[:, 0:ntj], in0=mu_t[:, 0:ntj], in1=mu_t[:, 0:ntj]))
        b.w(v, VEc)
        b.inc(VEc, v.tensor_sub(out=r_t[:, 0:ntj], in0=r_t[:, 0:ntj], in1=scr_t[:, 0:ntj]))
        b.inc(VEc, v.tensor_copy(out=csrow[0:1, 0:8], in_=psH[0:1, 0:8]))
    b.V(v1)

    def s2(s):
        b.w(s, VEc)
        b.inc(SEc, s.activation(out=r_t[:, 0:ntj], in_=r_t[:, 0:ntj], func=AF.Ln,
                                bias=epscol[:, 0:1]))
        b.w(s, SEc)
        b.inc(SEc, s.activation(out=r_t[:, 0:ntj], in_=r_t[:, 0:ntj], func=AF.Exp,
                                scale=nhalfcol[:, 0:1]))
    b.S(s2)

    def t2(t):
        b.w(t, VEc)
        b.inc(PEd, t.matmul(psH[:, 8:16], onerow[0:1, :], csrow[0:1, 0:8],
                            start=True, stop=True))
        b.w(t, SEc)
        b.inc(PEd, t.transpose(psK[0:ntj, 0:128], mu_t[:, 0:ntj], ident_f32[:]))
        b.inc(PEd, t.transpose(psK[0:ntj, 512:640], r_t[:, 0:ntj], ident_f32[:]))
    b.T(t2)

    def v2(v):
        b.w(v, PEd)
        b.inc(VEc, v.tensor_copy(out=csb[:, 0:8], in_=psH[:, 8:16]))
        b.inc(VEc, v.tensor_copy(out=trsc[0:ntj, 0:128], in_=psK[0:ntj, 0:128]))
        b.inc(VEc, v.tensor_copy(out=trsc[0:ntj, 128:256], in_=psK[0:ntj, 512:640]))
    b.V(v2)

    def q1(q):
        b.w(q, VEc)
        b.inc(LD2, q.dma_start(out=mrow[0:1, 0:nt].rearrange("o (a z) -> (o a) z", z=128),
                               in_=trsc[0:ntj, 0:128]), 16)
        b.inc(LD2, q.dma_start(out=mrow[1:2, 0:nt].rearrange("o (a z) -> (o a) z", z=128),
                               in_=trsc[0:ntj, 128:256]), 16)
        b.w(q, LD2)
        b.inc(LD2, q.dma_start(out=murow[0:1, 0:nt], in_=mrow[0:1, 0:nt]), 16)
        b.inc(LD2, q.dma_start(out=rrow[0:1, 0:nt], in_=mrow[1:2, 0:nt]), 16)
    b.Q(q1)

    def v3(v):
        b.w(v, VEc)
        b.inc(VEc, v.tensor_tensor(out=lscr[:, 0:ntj * 8],
                                   in0=mu_t[:, 0:ntj].unsqueeze(2).to_broadcast([128, ntj, 8]),
                                   in1=csb[:, 0:8].unsqueeze(1).to_broadcast([128, ntj, 8]),
                                   op=OP.mult))
        b.w(v, VEc)
        b.inc(VEc, v.tensor_tensor(out=lscr[:, 0:ntj * 8],
                                   in0=rawv[:, :, 0:8],
                                   in1=lscr[:, 0:ntj * 8].rearrange("p (a z) -> p a z", z=8),
                                   op=OP.subtract))
        b.w(v, VEc)
        b.w(v, SEc)
        b.inc(VEc, v.tensor_tensor(out=lscr[:, 0:ntj * 8],
                                   in0=lscr[:, 0:ntj * 8].rearrange("p (a z) -> p a z", z=8),
                                   in1=r_t[:, 0:ntj].unsqueeze(2).to_broadcast([128, ntj, 8]),
                                   op=OP.mult))
        b.w(v, VEc)
        b.inc(VEc, v.reduce_max(out=smax[:, 0:ntj],
                                in_=lscr[:, 0:ntj * 8].rearrange("p (a z) -> p a z", z=8),
                                axis=AX.X))
        b.w(v, VEc)
        b.inc(VEc, v.tensor_tensor(out=out_prob[:, 0:ntj * 8],
                                   in0=lscr[:, 0:ntj * 8].rearrange("p (a z) -> p a z", z=8),
                                   in1=smax[:, 0:ntj].unsqueeze(2).to_broadcast([128, ntj, 8]),
                                   op=OP.subtract))
    b.V(v3)

    def s3(s):
        b.w(s, VEc)
        b.inc(SEc, s.activation(out=out_prob[:, 0:ntj * 8], in_=out_prob[:, 0:ntj * 8],
                                func=AF.Exp))
    b.S(s3)

    def v4(v):
        b.w(v, SEc)
        b.inc(VEc, v.reduce_sum(out=smax[:, 0:ntj],
                                in_=out_prob[:, 0:ntj * 8].rearrange("p (a z) -> p a z", z=8),
                                axis=AX.X))
        b.w(v, VEc)
        b.inc(VEc, v.reciprocal(out=smax[:, 0:ntj], in_=smax[:, 0:ntj]))
        b.w(v, VEc)
        b.inc(VEc, v.tensor_tensor(out=out_prob[:, 0:ntj * 8],
                                   in0=out_prob[:, 0:ntj * 8].rearrange("p (a z) -> p a z", z=8),
                                   in1=smax[:, 0:ntj].unsqueeze(2).to_broadcast([128, ntj, 8]),
                                   op=OP.mult))
    b.V(v4)

    chunks = [(k, min(512, nt - k)) for k in range(0, nt, 512)]
    ve_base = b.cnt[VEc]
    pe_base = b.cnt[PEd]

    def t3(t):
        b.w(t, LD2)
        for i, (off, cw) in enumerate(chunks):
            if i >= 1:
                b.w(t, VEc, ve_base + i)
            t.matmul(psK[:, 0:cw], onerow[0:1, :], murow[0:1, off:off + cw],
                     start=True, stop=True)
            b.inc(PEd, t.matmul(psK[:, 512:512 + cw], onerow[0:1, :],
                                rrow[0:1, off:off + cw], start=True, stop=True))
    b.T(t3)

    def v5(v):
        for i, (off, cw) in enumerate(chunks):
            b.w(v, PEd, pe_base + i + 1)
            v.tensor_copy(out=mu_b[:, off:off + cw], in_=psK[:, 0:cw])
            b.inc(VEc, v.tensor_copy(out=r_b[:, off:off + cw], in_=psK[:, 512:512 + cw]))
    b.V(v5)

    def v6(v):
        b.w(v, VEc)
        for ci in range(NCI):
            if ci >= 2:
                b.dwait(v, XNS, ci - 2)
            b.w(v, VEc)
            b.inc(VEc, v.tensor_sub(out=tsub[:], in0=xs[ci][:], in1=mu_b[:, 0:nt]))
            b.w(v, VEc)
            b.inc(VEc, v.tensor_mul(out=xnp[ci % 2][:], in0=tsub[:], in1=r_b[:, 0:nt]))
    b.V(v6)
    ve_n = b.cnt[VEc]

    def q2(q):
        for ci in range(NCI):
            b.w(q, VEc, ve_n - 2 * (NCI - ci) + 2)
            b.dinc(XNS, ci, q.dma_start(out=out_xn_hbm[128 * ci:128 * (ci + 1), :],
                                        in_=xnp[ci % 2][:]))
    b.Q(q2)



def build_gate1():
    b = B(176 * 1024)
    nc = b.nc
    xh = nc.dram_tensor("xh", [C, TT], F32, kind="ExternalInput").ap()
    w9 = nc.dram_tensor("w9", [C, 9], F32, kind="ExternalInput").ap()
    ident = nc.dram_tensor("ident", [128, 128], F32, kind="ExternalInput").ap()
    wc_in = nc.dram_tensor("wc_in", [2560, 2048], F32, kind="ExternalInput").ap()
    o_xn = nc.dram_tensor("o_xn", [C, TT], BF16, kind="ExternalOutput").ap()
    o_pr = nc.dram_tensor("o_pr", [128, NTJ * 8], F32, kind="ExternalOutput").ap()
    o_wc = nc.dram_tensor("o_wc", [2560, 2048], BF16, kind="ExternalOutput").ap()
    mrow = nc.dram_tensor("mrow", [2, TT], F32, kind="Internal").ap()

    b.start()
    xs = [b.alloc(TT, F32) for _ in range(NCI)]
    w9s = b.alloc(9 * NCI, F32)
    onescol = b.alloc(1, F32)
    onerow = b.alloc(128, F32)
    epscol = b.alloc(1, F32)
    nhalfcol = b.alloc(1, F32)
    idf = b.alloc(128, F32)
    prob = b.alloc(NTJ * 8, F32)
    murow = b.alloc(TT, F32)
    rrow = b.alloc(TT, F32)
    mu_b = b.alloc(TT, F32)
    r_b = b.alloc(TT, F32)

    psG = b.ctx.enter_context(nc.psum_tensor("psG", [128, 512], F32))
    psH = b.ctx.enter_context(nc.psum_tensor("psH", [128, 512], F32))
    psK = b.ctx.enter_context(nc.psum_tensor("psK", [128, 1024], F32))

    def q0(q):
        for ci in range(NCI):
            b.inc("LD", q.dma_start(out=xs[ci][:], in_=xh[128 * ci:128 * ci + 128, :]), 16)
        for ci in range(NCI):
            b.inc("LD", q.dma_start(out=w9s[:, 9 * ci:9 * ci + 9],
                                    in_=w9[128 * ci:128 * ci + 128, :]), 16)
        b.inc("LD", q.dma_start(out=idf[:], in_=ident), 16)
    b.Q(q0)

    def g0(g):
        b.inc("WC", g.dma_start(out=o_wc, in_=wc_in), 16)
    b.G(g0)

    ldv = b.cnt["LD"]

    def v0(v):
        b.w(v, "LD", ldv)
        v.memset(onescol[:, 0:1], 1.0)
        v.memset(epscol[:, 0:1], LNEPS)
        v.memset(nhalfcol[:, 0:1], -0.5)
        b.inc("gVEc", v.memset(onerow[0:1, :], 1.0))
    b.V(v0)

    def s0(s):
        b.w(s, "LD", ldv)
        b.w(s, "gVEc", 1)
    b.S(s0)

    def t0(t):
        b.w(t, "LD", ldv)
        b.w(t, "gVEc", 1)
    b.T(t0)

    emit_ln_gate(b, xs, TT, idf, w9s, onescol, onerow, epscol, nhalfcol, psG, psH, psK,
                 o_xn, prob, mrow, murow, rrow, mu_b, r_b, "g")

    def q9(q):
        b.w(q, "gVEc")
        b.inc("ST", q.dma_start(out=o_pr, in_=prob[:, 0:NTJ * 8]), 16)
    b.Q(q9)

    def g9(g):
        b.w(g, "ST")
        b.w(g, "WC")
        b.dwait_all(g, "gXNS", NCI)
    b.G(g9)
    return b



def build_tau():
    b = B(32 * 1024)
    nc = b.nc
    pp = nc.dram_tensor("pp", [16384, 8], F32, kind="ExternalInput").ap()
    grid0 = nc.dram_tensor("grid0", [1, TAU_W], F32, kind="ExternalInput").ap()
    o_tau = nc.dram_tensor("o_tau", [1, 1], F32, kind="ExternalOutput").ap()

    b.start()
    pps = b.alloc(128 * 8, F32)
    th = b.alloc(TAU_W, F32)
    thb = b.alloc(TAU_W, F32)
    msk = b.alloc(TAU_W * 128, BF16)
    red = b.alloc(TAU_W, F32)
    cnt = b.alloc(TAU_W, F32)
    m8 = b.alloc(TAU_W, F32)
    scr8 = b.alloc(TAU_W, F32)
    lo = b.alloc(4, F32)
    hi = b.alloc(4, F32)
    span = b.alloc(4, F32)
    iot = b.alloc(TAU_W, F32)
    onerow = b.alloc(128, F32)
    onescol = b.alloc(1, F32)

    psA = b.ctx.enter_context(nc.psum_tensor("psA", [128, 512], F32))

    def q0(q):
        b.inc("LD", q.dma_start(out=pps[:].rearrange("p (a z) -> p a z", z=8),
                                in_=pp.rearrange("(p a) z -> p a z", a=128)), 16)
        b.inc("LD", q.dma_start(out=iot[0:1, 0:TAU_W], in_=grid0), 16)
    b.Q(q0)
    ldv = b.cnt["LD"]

    def v0(v):
        b.w(v, "LD", ldv)
        v.memset(onescol[:, 0:1], 1.0)
        v.memset(onerow[0:1, :], 1.0)
        v.memset(lo[0:1, 0:1], 0.0)
        v.memset(hi[0:1, 0:1], 1.0)
        b.inc("VE", v.tensor_scalar(out=th[0:1, 0:TAU_W], in0=iot[0:1, 0:TAU_W],
                                    scalar1=1.0 / (TAU_W + 1), scalar2=None, op0=OP.mult))
    b.V(v0)

    ppv = pps[:].rearrange("p (a z) -> p a z", z=8)

    for it in range(TAU_ITERS):
        def t1(t, it=it):
            b.w(t, "VE")
            b.inc("PE", t.matmul(psA[:, 0:TAU_W], onerow[0:1, :], th[0:1, 0:TAU_W],
                                 start=True, stop=True))
        b.T(t1)

        def v1(v, it=it):
            b.w(v, "PE")
            b.inc("VE", v.tensor_copy(out=thb[:, 0:TAU_W], in_=psA[:, 0:TAU_W]))
            b.w(v, "VE")
            b.inc("VE", v.tensor_tensor(
                out=msk[:, 0:TAU_W * 128].rearrange("p (w a) -> p w a", a=128),
                in0=ppv[:, :, 0:1].squeeze(2).unsqueeze(1).to_broadcast([128, TAU_W, 128]),
                in1=thb[:, 0:TAU_W].unsqueeze(2).to_broadcast([128, TAU_W, 128]),
                op=OP.is_ge))
            b.w(v, "VE")
            b.inc("VE", v.reduce_sum(out=red[:, 0:TAU_W],
                                     in_=msk[:, 0:TAU_W * 128].rearrange("p (w a) -> p w a", a=128),
                                     axis=AX.X))
        b.V(v1)

        def t2(t, it=it):
            b.w(t, "VE")
            b.inc("PE", t.matmul(psA[0:1, 64:64 + TAU_W], onescol[:, 0:1], red[:, 0:TAU_W],
                                 start=True, stop=True))
        b.T(t2)

        def v2(v, it=it):
            b.w(v, "PE")
            b.inc("VE", v.tensor_copy(out=cnt[0:1, 0:TAU_W], in_=psA[0:1, 64:64 + TAU_W]))
            b.w(v, "VE")
            b.inc("VE", v.tensor_scalar(out=m8[0:1, 0:TAU_W], in0=cnt[0:1, 0:TAU_W],
                                        scalar1=2048.0, scalar2=None, op0=OP.is_ge))
            b.w(v, "VE")
            b.inc("VE", v.tensor_mul(out=scr8[0:1, 0:TAU_W], in0=th[0:1, 0:TAU_W],
                                     in1=m8[0:1, 0:TAU_W]))
            b.w(v, "VE")
            b.inc("VE", v.reduce_max(out=scr8[0:1, 0:1], in_=scr8[0:1, 0:TAU_W], axis=AX.X))
            b.w(v, "VE")
            b.inc("VE", v.tensor_max(out=lo[0:1, 0:1], in0=lo[0:1, 0:1], in1=scr8[0:1, 0:1]))
            b.w(v, "VE")
            b.inc("VE", v.scalar_tensor_tensor(out=scr8[0:1, 0:TAU_W], in0=m8[0:1, 0:TAU_W],
                                               scalar=BIG, in1=th[0:1, 0:TAU_W],
                                               op0=OP.mult, op1=OP.add))
            b.w(v, "VE")
            b.inc("VE", v.tensor_reduce(out=scr8[0:1, 1:2], in_=scr8[0:1, 0:TAU_W],
                                        axis=AX.X, op=OP.min))
            b.w(v, "VE")
            b.inc("VE", v.tensor_tensor(out=hi[0:1, 0:1], in0=hi[0:1, 0:1],
                                        in1=scr8[0:1, 1:2], op=OP.min))
            b.w(v, "VE")
            b.inc("VE", v.tensor_scalar(out=span[0:1, 0:1], in0=hi[0:1, 0:1],
                                        scalar1=lo[0:1, 0:1], scalar2=1.0 / (TAU_W + 1),
                                        op0=OP.subtract, op1=OP.mult))
            b.w(v, "VE")
            b.inc("VE", v.tensor_scalar(out=th[0:1, 0:TAU_W], in0=iot[0:1, 0:TAU_W],
                                        scalar1=span[0:1, 0:1], scalar2=lo[0:1, 0:1],
                                        op0=OP.mult, op1=OP.add))
        b.V(v2)

    def q9(q):
        b.w(q, "VE")
        b.inc("ST", q.dma_start(out=o_tau, in_=lo[0:1, 0:1]), 16)
    b.Q(q9)

    def g9(g):
        b.w(g, "ST", 16)
    b.G(g9)
    return b



def emit_routing(b, ntj, prob, tau_row, iota8_row, iotab_row, onescol, onerow,
                 ut128, psR, psT, psB, gate, idx, pfx, dbg=None):
    RV, RP = pfx + "RV", pfx + "RP"
    A = b.alloc(ntj * 8, F32)
    Cc = b.alloc(ntj * 8, F32)
    asn = b.alloc(ntj, F32)
    mB = b.alloc(ntj, F32)
    t2e = b.alloc(ntj, F32)
    rank = b.alloc(ntj, F32)
    taub = b.alloc(8, F32)
    iotab = b.alloc(8, F32)
    iotBb = b.alloc(8, F32)
    off = b.alloc(8 * ntj + 8, F32)
    slotf = b.alloc(ntj, F32)

    pv = prob[:, 0:ntj * 8].rearrange("p (a z) -> p a z", z=8)
    Av = A[:, 0:ntj * 8].rearrange("p (a z) -> p a z", z=8)
    Cv = Cc[:, 0:ntj * 8].rearrange("p (a z) -> p a z", z=8)

    def t0(t):
        b.inc(RP, t.matmul(psB[:, 0:24], onerow[0:1, :], tau_row[0:1, 0:24],
                           start=True, stop=True))
    b.T(t0)

    def v1(v):
        b.w(v, RP)
        v.tensor_copy(out=taub[:, 0:8], in_=psB[:, 0:8])
        v.tensor_copy(out=iotab[:, 0:8], in_=psB[:, 8:16])
        b.inc(RV, v.tensor_copy(out=iotBb[:, 0:8], in_=psB[:, 16:24]))
        b.w(v, RV)
        b.inc(RV, v.tensor_tensor(out=Av, in0=pv,
                                  in1=taub[:, 0:8].unsqueeze(1).to_broadcast([128, ntj, 8]),
                                  op=OP.is_ge))
        b.w(v, RV)
        b.inc(RV, v.reduce_max(out=asn[:, 0:ntj], in_=Av, axis=AX.X))
        b.w(v, RV)
        b.inc(RV, v.tensor_tensor(out=Av, in0=pv, in1=Av, op=OP.mult))
        b.w(v, RV)
        b.inc(RV, v.tensor_scalar(out=mB[:, 0:ntj], in0=asn[:, 0:ntj], scalar1=-1.0,
                                  scalar2=1.0, op0=OP.mult, op1=OP.add))
        b.w(v, RV)
        b.inc(RV, v.tensor_tensor(out=Cv, in0=pv,
                                  in1=mB[:, 0:ntj].unsqueeze(2).to_broadcast([128, ntj, 8]),
                                  op=OP.mult))
        b.w(v, RV)
        b.inc(RV, v.tensor_tensor(out=Cv, in0=Cv, in1=Av, op=OP.add))
        b.w(v, RV)
        b.inc(RV, v.reduce_max(out=mB[:, 0:ntj], in_=Cv, axis=AX.X))
        b.w(v, RV)
        b.inc(RV, v.tensor_tensor(out=Cv, in0=Cv,
                                  in1=mB[:, 0:ntj].unsqueeze(2).to_broadcast([128, ntj, 8]),
                                  op=OP.is_equal))
        b.w(v, RV)
        b.inc(RV, v.scalar_tensor_tensor(out=Cv, in0=Cv, scalar=-1024.0,
                                         in1=iotBb[:, 0:8].unsqueeze(1).to_broadcast([128, ntj, 8]),
                                         op0=OP.mult, op1=OP.add))
        b.w(v, RV)
        b.inc(RV, v.tensor_reduce(out=t2e[:, 0:ntj], in_=Cv, axis=AX.X, op=OP.min))
        b.w(v, RV)
        b.inc(RV, v.tensor_tensor(out=Cv,
                                  in0=iotab[:, 0:8].unsqueeze(1).to_broadcast([128, ntj, 8]),
                                  in1=t2e[:, 0:ntj].unsqueeze(2).to_broadcast([128, ntj, 8]),
                                  op=OP.is_equal))
        b.w(v, RV)
        b.inc(RV, v.tensor_tensor(out=Av, in0=pv, in1=Cv, op=OP.mult))
        b.w(v, RV)
        b.inc(RV, v.reduce_sum(out=mB[:, 0:ntj], in_=Av, axis=AX.X))
        b.w(v, RV)
        b.inc(RV, v.tensor_scalar(out=gate[:, 0:ntj], in0=mB[:, 0:ntj], scalar1=0.5,
                                  scalar2=None, op0=OP.mult))
    b.V(v1)

    rv_onehot = b.cnt[RV]
    Crank = b.alloc(ntj * 8, F32)
    tsrow = b.alloc(ntj * 8, F32)
    offb = b.alloc(ntj * 8, F32)
    rp0 = b.cnt[RP]

    def t1(t):
        b.w(t, RV, rv_onehot)
        for tj in range(ntj):
            if tj >= 2:
                b.w(t, RV, rv_onehot + tj - 1)
            b.inc(RP, t.matmul([psR, psT][tj % 2][:, 0:8], ut128[:], Cv[:, tj, :],
                               start=True, stop=True))
    b.T(t1)

    def v1b(v):
        for tj in range(ntj):
            b.w(v, RP, rp0 + tj + 1)
            b.inc(RV, v.tensor_copy(out=Crank[:, 8 * tj:8 * tj + 8],
                                    in_=[psR, psT][tj % 2][:, 0:8]))
    b.V(v1b)

    rv_crank = b.cnt[RV]
    rp_crank = b.cnt[RP]

    def t1c(t):
        for tj in range(ntj):
            if tj >= 2:
                b.w(t, RV, rv_crank + tj - 1)
            else:
                b.w(t, RV, rv_crank)
            b.inc(RP, t.matmul([psR, psT][tj % 2][0:1, 0:8], onescol[:, 0:1],
                               Cv[:, tj, :], start=True, stop=True))
    b.T(t1c)

    def v2(v):
        for tj in range(ntj):
            b.w(v, RP, rp_crank + tj + 1)
            b.inc(RV, v.tensor_copy(out=tsrow[0:1, 8 * tj:8 * tj + 8],
                                    in_=[psR, psT][tj % 2][0:1, 0:8]))
        b.inc(RV, v.memset(off[0:1, 0:8], 0.0))
        for j in range(1, ntj):
            b.w(v, RV)
            b.inc(RV, v.tensor_tensor(out=off[0:1, 8 * j:8 * j + 8],
                                      in0=off[0:1, 8 * (j - 1):8 * j],
                                      in1=tsrow[0:1, 8 * (j - 1):8 * j], op=OP.add))
    b.V(v2)

    def t2(t):
        b.w(t, pfx + "RV")
        b.inc(RP, t.matmul(psB[:, 0:8 * ntj], onerow[0:1, :], off[0:1, 0:8 * ntj],
                           start=True, stop=True))
    b.T(t2)

    def v2b(v):
        b.w(v, RP)
        b.inc(RV, v.tensor_copy(out=offb[:, 0:8 * ntj], in_=psB[:, 0:8 * ntj]))
    b.V(v2b)

    if dbg is not None:
        def q_dbg(q):
            b.w(q, RV)
            b.inc("DBGR", q.dma_start(out=dbg["taub"], in_=taub[:, 0:8]), 16)
            b.inc("DBGR", q.dma_start(out=dbg["iotab"], in_=iotab[:, 0:8]), 16)
            b.inc("DBGR", q.dma_start(out=dbg["iotBb"], in_=iotBb[:, 0:8]), 16)
            b.inc("DBGR", q.dma_start(out=dbg["t2e"], in_=t2e[:, 0:ntj]), 16)
            b.inc("DBGR", q.dma_start(out=dbg["asn"], in_=asn[:, 0:ntj]), 16)
            b.inc("DBGR", q.dma_start(out=dbg["crank"], in_=Crank[:, 0:8 * ntj]), 16)
            b.inc("DBGR", q.dma_start(out=dbg["offb"], in_=offb[:, 0:8 * ntj]), 16)
        b.Q(q_dbg)

    def v3(v):
        b.w(v, RV)
        if dbg is not None:
            b.w(v, "DBGR")
        b.inc(RV, v.tensor_tensor(out=offb[:, 0:8 * ntj], in0=offb[:, 0:8 * ntj],
                                  in1=Crank[:, 0:8 * ntj], op=OP.add))
        b.w(v, RV)
        b.inc(RV, v.tensor_tensor(out=Av, in0=Cv,
                                  in1=offb[:, 0:8 * ntj].rearrange("p (a z) -> p a z", z=8),
                                  op=OP.mult))
        b.w(v, RV)
        b.inc(RV, v.reduce_sum(out=rank[:, 0:ntj], in_=Av, axis=AX.X))
        b.w(v, RV)
        b.inc(RV, v.scalar_tensor_tensor(out=slotf[:, 0:ntj], in0=t2e[:, 0:ntj],
                                         scalar=float(CAP), in1=rank[:, 0:ntj],
                                         op0=OP.mult, op1=OP.add))
        b.w(v, RV)
        b.inc(RV, v.tensor_scalar(out=mB[:, 0:ntj], in0=rank[:, 0:ntj],
                                  scalar1=float(CAP), scalar2=None, op0=OP.is_ge))
        b.w(v, RV)
        b.inc(RV, v.scalar_tensor_tensor(out=slotf[:, 0:ntj], in0=mB[:, 0:ntj],
                                         scalar=65536.0, in1=slotf[:, 0:ntj],
                                         op0=OP.mult, op1=OP.add))
        b.w(v, RV)
        b.inc(RV, v.tensor_copy(out=idx[:, 0:ntj], in_=slotf[:, 0:ntj]))
    b.V(v3)



def emit_ffn(b, ntj, xn_hbm, idx, w1b, w2b, frame, retf,
             ident_f32, psW0, psW1, psO0, psO1, pfx, fstage=3):
    nc = b.nc
    TPS, SCT, XES, WL1, WL2, FPE, GEV, OEV, RST = (
        pfx + s for s in ["TPS", "SCT", "XES", "WL1", "WL2", "FPE",
                          "GEV", "OEV", "RST"])
    xtm = [b.alloc(1024, BF16) for _ in range(2)]
    x_e = [b.alloc(SLOTS, BF16) for _ in range(NCI)]
    w1sb = b.alloc(NCI * 2048, BF16)
    w2sb = [b.alloc(NFI * 1024, BF16) for _ in range(2)]
    gh = b.alloc(NFI * CAP, BF16)
    otm = [b.alloc(1024, BF16) for _ in range(2)]

    def v_z(v):
        b.inc(pfx + "FRV", v.memset(gh[:, 0:1024], 0.0))
    b.V(v_z)

    def q_z(q):
        b.w(q, pfx + "FRV", b.cnt[pfx + "FRV"])
        for j in range(SLOTS // 128):
            b.inc(pfx + "FRZ", q.dma_start(out=frame[128 * j:128 * (j + 1), :],
                                           in_=gh[:, 0:1024].bitcast(BF16)
                                           if False else gh[:, 0:1024]), 16)
    b.G(q_z)

    def q0(q):
        for tj in range(ntj):
            if tj >= 2:
                b.dwait(q, SCT, tj - 2)
            b.dinc(TPS, tj, q.dma_start(out=xtm[tj % 2][:],
                                        in_=xn_hbm[:, 128 * tj:128 * tj + 128],
                                        transpose=True))
    b.Q(q0)

    def g0(g):
        b.w(g, "rRV")
        b.w(g, pfx + "FRZ", 16 * (SLOTS // 128))
        for tj in range(ntj):
            b.dwait(g, TPS, tj)
            if tj >= 1:
                b.dwait(g, SCT, tj - 1)
            b.dinc(SCT, tj, g.indirect_dma_start(
                out=frame, out_offset=bass.IndirectOffsetOnAxis(idx[:, tj:tj + 1], 0),
                in_=xtm[tj % 2][:], in_offset=None,
                bounds_check=SLOTS - 1, oob_is_err=False))
    b.G(g0)

    if fstage <= 1:
        return

    def q1(s):
        b.dwait_all(s, SCT, ntj)
        for ci in range(NCI):
            b.inc(XES, s.dma_start(out=x_e[ci][:],
                                   in_=frame[:, 128 * ci:128 * ci + 128],
                                   transpose=True), 16)
    b.S(q1)

    if fstage <= 2:
        return

    def q23(q):
        for ci in range(NCI):
            b.dinc(WL1, 0, q.dma_start(out=w1sb[:, 2048 * ci:2048 * (ci + 1)],
                                       in_=w1b[128 * ci:128 * (ci + 1), :]))
        for e in range(E):
            if e + 1 < E:
                b.w(q, FPE, e * 19 + 16)
                for ci in range(NCI):
                    b.dinc(WL1, e + 1, q.dma_start(
                        out=w1sb[:, 2048 * ci:2048 * (ci + 1)],
                        in_=w1b[1024 * (e + 1) + 128 * ci:1024 * (e + 1) + 128 * (ci + 1), :]))
            for tc in range(3):
                b.w(q, OEV, e * 3 + tc + 1)
                b.dinc(RST, e * 3 + tc, q.dma_start(
                    out=retf[CAP * e + 128 * tc:CAP * e + 128 * (tc + 1), :],
                    in_=otm[(e * 3 + tc) % 2][:]))
    b.Q(q23)

    def g23(g):
        for e in range(E):
            if e >= 2:
                b.w(g, FPE, (e - 1) * 19)
            for fi in range(NFI):
                b.dinc(WL2, e, g.dma_start(
                    out=w2sb[e % 2][:, 1024 * fi:1024 * (fi + 1)],
                    in_=w2b[2048 * e + 128 * fi:2048 * e + 128 * (fi + 1), :]))
    b.G(g23)

    def t1(t):
        b.w(t, XES)
        for e in range(E):
            b.dwait(t, WL1, e, per=128)
            psW = [psW0, psW1]
            for fi in range(NFI):
                need = e * NFI + fi - 1
                if need >= 1:
                    b.w(t, GEV, need)
                for ci in range(NCI):
                    ii = t.matmul(psW[fi % 2][:, 0:CAP],
                                  w1sb[:, 2048 * ci + 128 * fi:2048 * ci + 128 * fi + 128],
                                  x_e[ci][:, CAP * e:CAP * (e + 1)],
                                  start=(ci == 0), stop=(ci == NCI - 1),
                                  skip_group_check=True)
                b.inc(FPE, ii)
            b.dwait(t, WL2, e, per=256)
            b.w(t, GEV, (e + 1) * NFI)
            psO = [psO0, psO1]
            for tc in range(3):
                need = e * 3 + tc - 1
                if need >= 1:
                    b.w(t, OEV, need)
                for fi in range(NFI):
                    lhs = gh[:, CAP * fi + 128 * tc:CAP * fi + 128 * tc + 128]
                    t.matmul(psO[tc % 2][:, 0:512], lhs,
                             w2sb[e % 2][:, 1024 * fi:1024 * fi + 512],
                             start=(fi == 0), stop=(fi == NFI - 1), skip_group_check=True)
                    ii = t.matmul(psO[tc % 2][:, 512:1024], lhs,
                                  w2sb[e % 2][:, 1024 * fi + 512:1024 * fi + 1024],
                                  start=(fi == 0), stop=(fi == NFI - 1), skip_group_check=True)
                b.inc(FPE, ii)
    b.T(t1)

    def s1(s):
        for e in range(E):
            for fi in range(NFI):
                b.w(s, FPE, e * 19 + fi + 1)
                b.inc(GEV, s.activation(out=gh[:, CAP * fi:CAP * (fi + 1)],
                                        in_=psW0[:, 0:CAP] if fi % 2 == 0 else psW1[:, 0:CAP],
                                        func=AF.Gelu))
    b.S(s1)

    def v1(v):
        for e in range(E):
            for tc in range(3):
                k = e * 3 + tc
                if k >= 2:
                    b.dwait(v, RST, k - 2)
                b.w(v, FPE, e * 19 + NFI + tc + 1)
                b.inc(OEV, v.tensor_copy(out=otm[k % 2][:],
                                          in_=psO0[:, 0:1024] if tc % 2 == 0 else psO1[:, 0:1024]))
    b.V(v1)




TOKCH = 768
NCH = TT // TOKCH


def build_main(stage=4, FSTAGE=3, DBG=False):
    b = B(176 * 1024)
    nc = b.nc
    xh = nc.dram_tensor("xh", [C, TT], F32, kind="ExternalInput").ap()
    xn1h = nc.dram_tensor("xn1h", [C, TT], BF16, kind="ExternalInput").ap()
    pr1 = nc.dram_tensor("pr1", [128, NTJ * 8], F32, kind="ExternalInput").ap()
    tau1 = nc.dram_tensor("tau1", [1, 8], F32, kind="ExternalInput").ap()
    w9 = nc.dram_tensor("w9", [C, 9], F32, kind="ExternalInput").ap()
    ident = nc.dram_tensor("ident", [128, 128], F32, kind="ExternalInput").ap()
    ut = nc.dram_tensor("ut", [128, 128], F32, kind="ExternalInput").ap()
    iota8 = nc.dram_tensor("iota8", [1, 8], F32, kind="ExternalInput").ap()
    iotaB = nc.dram_tensor("iotaB", [1, 8], F32, kind="ExternalInput").ap()
    hmask = nc.dram_tensor("hmask", [1, HALO], F32, kind="ExternalInput").ap()
    mixk = nc.dram_tensor("mixk", [128, 8], F32, kind="ExternalInput").ap()
    mixv = nc.dram_tensor("mixv", [128, 8], F32, kind="ExternalInput").ap()
    mixr = nc.dram_tensor("mixr", [128, 8], F32, kind="ExternalInput").ap()
    tdk = nc.dram_tensor("tdk", [128, 8], F32, kind="ExternalInput").ap()
    tfk = nc.dram_tensor("tfk", [128, 8], F32, kind="ExternalInput").ap()
    lnwk = nc.dram_tensor("lnwk", [128, 8], F32, kind="ExternalInput").ap()
    lnbk = nc.dram_tensor("lnbk", [128, 8], F32, kind="ExternalInput").ap()
    w1b = nc.dram_tensor("w1b", [8 * C, FF], BF16, kind="ExternalInput").ap()
    w2b = nc.dram_tensor("w2b", [8 * FF, C], BF16, kind="ExternalInput").ap()
    wkb = nc.dram_tensor("wkb", [C, C], BF16, kind="ExternalInput").ap()
    wvb = nc.dram_tensor("wvb", [C, C], BF16, kind="ExternalInput").ap()
    wrb = nc.dram_tensor("wrb", [C, C], BF16, kind="ExternalInput").ap()
    wob = nc.dram_tensor("wob", [C, C], BF16, kind="ExternalInput").ap()

    o_x2 = nc.dram_tensor("o_x2", [C, TOWN], F32, kind="ExternalOutput").ap()
    o_xn2 = nc.dram_tensor("o_xn2", [C, TOWN], BF16, kind="ExternalOutput").ap()
    o_pr2 = nc.dram_tensor("o_pr2", [128, (TOWN // 128) * 8], F32, kind="ExternalOutput").ap()

    frame = nc.dram_tensor("frame", [SLOTS, C], BF16, kind="Internal").ap()
    retf = nc.dram_tensor("retf", [SLOTS, C], BF16, kind="Internal").ap()
    tmscr = nc.dram_tensor("tmscr", [TT, C], BF16, kind="Internal").ap()
    x1scr = nc.dram_tensor("x1scr", [C, TT], F32, kind="Internal").ap()
    wspill = nc.dram_tensor("wspill", [C, TT], F32, kind="Internal").ap()
    mrow = nc.dram_tensor("mrow", [2, TT], F32, kind="Internal").ap()
    if DBG:
        rdbg = {k: nc.dram_tensor("o_r" + k, [128, w], F32, kind="ExternalOutput").ap()
                for k, w in [("taub", 8), ("iotab", 8), ("iotBb", 8), ("t2e", NTJ),
                             ("asn", NTJ), ("crank", NTJ * 8), ("offb", NTJ * 8)]}
        o_dx1 = nc.dram_tensor("o_dx1", [C, TT], F32, kind="ExternalOutput").ap()
        o_dw = nc.dram_tensor("o_dw", [C, TT], F32, kind="ExternalOutput").ap()
        o_dk = nc.dram_tensor("o_dk", [C, TOKCH], F32, kind="ExternalOutput").ap()
        o_dv = nc.dram_tensor("o_dv", [C, TOKCH], F32, kind="ExternalOutput").ap()
        o_dr = nc.dram_tensor("o_dr", [C, TOKCH], F32, kind="ExternalOutput").ap()
        o_dxt = nc.dram_tensor("o_dxt", [C, TOKCH], F32, kind="ExternalOutput").ap()
        o_didx = nc.dram_tensor("o_didx", [128, NTJ], I32, kind="ExternalOutput").ap()
        o_dgate = nc.dram_tensor("o_dgate", [128, NTJ], F32, kind="ExternalOutput").ap()

    b.start()
    PS_A = b.ctx.enter_context(nc.psum_tensor("PS_A", [128, 512], F32))
    PS_B = b.ctx.enter_context(nc.psum_tensor("PS_B", [128, 512], F32))
    PS_C = b.ctx.enter_context(nc.psum_tensor("PS_C", [128, 512], F32))
    PS_D = b.ctx.enter_context(nc.psum_tensor("PS_D", [128, 512], F32))
    PS_E = b.ctx.enter_context(nc.psum_tensor("PS_E", [128, 1024], F32))
    PS_F = b.ctx.enter_context(nc.psum_tensor("PS_F", [128, 1024], F32))

    prob = b.alloc(NTJ * 8, F32)
    gate = b.alloc(NTJ, F32)
    idx = b.alloc(NTJ, I32)
    trow = b.alloc(24, F32)
    taur = trow[:, 0:8]
    io8r = trow[:, 8:16]
    ioBr = trow[:, 16:24]
    onescol = b.alloc(1, F32)
    onerow = b.alloc(128, F32)
    onescol_bf = b.alloc(1, BF16)
    epscol = b.alloc(1, F32)
    nhalfcol = b.alloc(1, F32)
    idf = b.alloc(128, F32)
    utsb = b.alloc(128, F32)
    mixs = {"k": b.alloc(8, F32), "v": b.alloc(8, F32), "r": b.alloc(8, F32)}
    mix1m = {"k": b.alloc(8, F32), "v": b.alloc(8, F32), "r": b.alloc(8, F32)}
    dk = b.alloc(8, F32)
    tf = b.alloc(8, F32)
    lnw = b.alloc(8, F32)
    lnb = b.alloc(8, F32)
    maskb = b.alloc(HALO, F32)
    carry = b.alloc(8, F32)
    bndA = b.alloc(8, BF16)
    bndB = b.alloc(8, BF16)

    def q_init(q):
        b.inc("LD", q.dma_start(out=prob[:, :], in_=pr1), 16)
        b.inc("LD", q.dma_start(out=taur[0:1, 0:8], in_=tau1), 16)
        b.inc("LD", q.dma_start(out=io8r[0:1, 0:8], in_=iota8), 16)
        b.inc("LD", q.dma_start(out=ioBr[0:1, 0:8], in_=iotaB), 16)
        b.inc("LD", q.dma_start(out=idf[:], in_=ident), 16)
        b.inc("LD", q.dma_start(out=utsb[:], in_=ut), 16)
        b.inc("LD", q.dma_start(out=mixs["k"][:, 0:8], in_=mixk), 16)
        b.inc("LD", q.dma_start(out=mixs["v"][:, 0:8], in_=mixv), 16)
        b.inc("LD", q.dma_start(out=mixs["r"][:, 0:8], in_=mixr), 16)
        b.inc("LD", q.dma_start(out=dk[:, 0:8], in_=tdk), 16)
        b.inc("LD", q.dma_start(out=tf[:, 0:8], in_=tfk), 16)
        b.inc("LD", q.dma_start(out=lnw[:, 0:8], in_=lnwk), 16)
        b.inc("LD", q.dma_start(out=lnb[:, 0:8], in_=lnbk), 16)
    b.Q(q_init)
    ld0 = b.cnt["LD"]

    def v_init(v):
        b.w(v, "LD", ld0)
        v.memset(onescol[:, 0:1], 1.0)
        v.memset(onescol_bf[:, 0:1], 1.0)
        v.memset(epscol[:, 0:1], LNEPS)
        v.memset(nhalfcol[:, 0:1], -0.5)
        v.memset(onerow[0:1, :], 1.0)
        v.memset(carry[:, 0:8], 0.0)
        v.memset(bndA[:, 0:8], 0.0)
        for p in ("k", "v", "r"):
            v.tensor_scalar(out=mix1m[p][:, 0:8], in0=mixs[p][:, 0:8],
                            scalar1=-1.0, scalar2=1.0, op0=OP.mult, op1=OP.add)
        b.inc("INI", v.engine_nop())
    b.V(v_init)

    def s_init(s):
        b.w(s, "INI", 1)
        s.activation(out=dk[:, 0:8], in_=dk[:, 0:8], func=AF.Sigmoid)
        b.inc("INI2", s.activation(out=tf[:, 0:8], in_=tf[:, 0:8], func=AF.Exp))
    b.S(s_init)

    def q_mask(q):
        b.inc("LDM", q.dma_start(out=maskb[0:1, 0:HALO], in_=hmask), 16)
    b.Q(q_mask)
    ld1 = b.cnt["LD"]

    maskbc = b.alloc(HALO, F32)

    def t_maskb(t):
        b.w(t, "LD", ld0)
        b.w(t, "LDM", 16)
        b.w(t, "INI", 1)
        b.inc("MKP", t.matmul(PS_F[:, 0:HALO], onerow[0:1, :], maskb[0:1, 0:HALO],
                              start=True, stop=True))
    b.T(t_maskb)

    def v_maskb(v):
        b.w(v, "MKP", 1)
        b.inc("MKV", v.tensor_copy(out=maskbc[:, 0:HALO], in_=PS_F[:, 0:HALO]))
    b.V(v_maskb)

    def pre_r(eng):
        b.w(eng, "LD", ld0)
        b.w(eng, "INI", 1)
        b.w(eng, "MKV", 1)
    for sec in (b.V, b.S, b.T, b.G, b.Q):
        sec(pre_r)

    if stage <= -1:
        def g_finm1(g):
            b.w(g, "MKV", 1)
        b.G(g_finm1)
        return b
    emit_routing(b, NTJ, prob, trow, io8r, ioBr, onescol, onerow, utsb,
                 PS_C, PS_D, PS_A, gate, idx, "r", dbg=(rdbg if DBG else None))
    if DBG:
        def q_dbgr(q):
            b.w(q, "rRV")
            b.inc("DBG1", q.dma_start(out=o_didx, in_=idx[:, 0:NTJ]), 16)
            b.inc("DBG1", q.dma_start(out=o_dgate, in_=gate[:, 0:NTJ]), 16)
        b.Q(q_dbgr)
    if stage <= 0:
        def g_fin0(g):
            b.w(g, "rRV")
        b.G(g_fin0)
        return b
    b.push()
    emit_ffn(b, NTJ, xn1h, idx, w1b, w2b, frame, retf,
             idf, PS_A, PS_B, PS_E, PS_F, "f", fstage=FSTAGE)
    b.pop()

    if stage <= 1:
        def g_fin1(g):
            if FSTAGE >= 3:
                b.dwait_all(g, "fRST", E * 3)
            elif FSTAGE == 2:
                b.w(g, "fXES")
            else:
                b.dwait_all(g, "fSCT", NTJ)
        b.G(g_fin1)
        return b

    b.push()
    xht = [b.alloc(TT, F32) for _ in range(2)]
    ffnt = [b.alloc(TT, BF16) for _ in range(2)]
    x1t = [b.alloc(TT, F32) for _ in range(2)]
    gtm = [b.ctx.enter_context(b.nc.sbuf_tensor("gtm%d" % i, [128, 1024], BF16)).ap()
           for i in range(2)]

    gts = [b.alloc(1024, BF16) for _ in range(2)]

    def g_gb(g):
        b.w(g, "fRSTA")
        b.w(g, "fRSTB")
        for tj in range(NTJ):
            if tj >= 2:
                b.w(g, "GBV", tj - 1)
            b.dinc("GBK", tj, g.indirect_dma_start(
                out=gtm[tj % 2][:], out_offset=None,
                in_=retf, in_offset=bass.IndirectOffsetOnAxis(idx[:, tj:tj + 1], 0),
                bounds_check=SLOTS - 1, oob_is_err=False))
    b.G(g_gb)

    def v_gb(v):
        for tj in range(NTJ):
            b.dwait(v, "GBK", tj)
            if tj >= 2:
                b.dwait(v, "GBS", tj - 2)
            b.inc("GBV", v.tensor_scalar(out=gts[tj % 2][:], in0=gtm[tj % 2][:],
                                         scalar1=gate[:, tj:tj + 1], scalar2=None,
                                         op0=OP.mult))
    b.V(v_gb)

    def q_gb2(q):
        for tj in range(NTJ):
            b.w(q, "GBV", tj + 1)
            b.dinc("GBS", tj, q.dma_start(out=tmscr[128 * tj:128 * (tj + 1), :],
                                          in_=gts[tj % 2][:]))
    b.Q(q_gb2)

    def q_x1(q):
        b.dwait_all(q, "GBS", NTJ)
        for ci in range(NCI + 1):
            if ci < NCI:
                if ci >= 2:
                    b.w(q, "X1V", ci - 1)
                q.dma_start(out=ffnt[ci % 2][:],
                            in_=tmscr[:, 128 * ci:128 * (ci + 1)],
                            transpose=True).then_inc(b.sem("X1T" + "AB"[ci % 2]), 16)
                b.dinc("X1T", ci, q.dma_start(out=xht[ci % 2][:],
                                              in_=xh[128 * ci:128 * (ci + 1), :]))
                b.cnt["X1T" + "AB"[ci % 2]] += 16
            if ci >= 1:
                b.w(q, "X1V", ci)
                b.dinc("X1S", ci - 1, q.dma_start(
                    out=x1scr[128 * (ci - 1):128 * ci, :], in_=x1t[(ci - 1) % 2][:]))
                if DBG:
                    b.inc("DBG1", q.dma_start(
                        out=o_dx1[128 * (ci - 1):128 * ci, :], in_=x1t[(ci - 1) % 2][:]), 16)
    b.Q(q_x1)

    def v_x1(v):
        for ci in range(NCI):
            b.dwait(v, "X1T", ci, per=32)
            if ci >= 2:
                b.dwait(v, "X1S", ci - 2)
            b.inc("X1V", v.tensor_tensor(out=x1t[ci % 2][:], in0=xht[ci % 2][:],
                                         in1=ffnt[ci % 2][:], op=OP.add))
    b.V(v_x1)

    if stage <= 2:
        def g_fin2(g):
            b.dwait_all(g, "X1S", NCI)
        b.G(g_fin2)
        return b

    b.pop()
    b.push()
    xt = [b.alloc(TOKCH, BF16) for _ in range(NCI)]
    vb = [b.alloc(TOKCH, BF16) for _ in range(NCI)]
    mx = [b.alloc(TOKCH, BF16) for _ in range(NCI)]
    K = [b.alloc(TOKCH, BF16) for _ in range(NCI)]
    Vt = [b.alloc(TOKCH, BF16) for _ in range(NCI)]
    R = [b.alloc(TOKCH, BF16) for _ in range(NCI)]
    Ssc = b.alloc(TOKCH, F32)
    wproj = [b.alloc(NCI * 1024, BF16) for _ in range(2)]
    st_in = [b.alloc(TOKCH, F32) for _ in range(2)]
    st_bf = [b.alloc(TOKCH, BF16) for _ in range(2)]
    st_sq = [b.alloc(TOKCH, BF16) for _ in range(2)]
    vf32 = b.alloc(TOKCH, F32)
    wtile = [b.alloc(TOKCH, F32) for _ in range(2)]
    srow = b.alloc(TOKCH, F32)
    qrow = b.alloc(TOKCH, F32)
    mu_bc = b.alloc(TOKCH, F32)
    r_bc = b.alloc(TOKCH, F32)
    x2t = [b.alloc(TOKCH, F32) for _ in range(2)]

    SCH = [(0, 512), (512, TOKCH - 512)]

    def stats_from(b, sfx, src_ready, n_ready_per_ci, bf_tiles, sq_tiles, src_base):
        for _n in (sfx + "P", sfx + "V", sfx + "S", src_ready):
            b.sem(_n)
        B0 = {n: b.cnt[n] for n in (sfx + "P", sfx + "V", sfx + "S", src_ready)}
        B0[src_ready] = src_base
        def tS(t):
            first = [True] * 4
            last = None
            for ci in range(NCI):
                b.w(t, src_ready, B0[src_ready] + n_ready_per_ci * (ci + 1))
                for kk, (off, cw) in enumerate(SCH):
                    sreg = (PS_A if kk == 0 else PS_B)[0:1, 0:cw]
                    qreg = PS_E[0:1, 0:cw] if kk == 0 else PS_E[0:1, 512:512 + cw]
                    t.matmul(sreg, onescol_bf[:, 0:1], bf_tiles(ci)[:, off:off + cw],
                             start=first[kk], stop=(ci == NCI - 1), skip_group_check=True)
                    first[kk] = False
                    last = t.matmul(qreg, onescol_bf[:, 0:1], sq_tiles(ci)[:, off:off + cw],
                                    start=first[kk + 2], stop=(ci == NCI - 1),
                                    skip_group_check=True)
                    first[kk + 2] = False
                b.inc(sfx + "P", last)
        b.T(tS)

        def vS(v):
            b.w(v, sfx + "P", B0[sfx + "P"] + NCI)
            for kk, (off, cw) in enumerate(SCH):
                sreg = (PS_A if kk == 0 else PS_B)[0:1, 0:cw]
                qreg = PS_E[0:1, 0:cw] if kk == 0 else PS_E[0:1, 512:512 + cw]
                v.tensor_scalar(out=srow[0:1, off:off + cw], in0=sreg,
                                scalar1=1.0 / C, scalar2=None, op0=OP.mult)
                b.inc(sfx + "V", v.tensor_scalar(out=qrow[0:1, off:off + cw], in0=qreg,
                                                 scalar1=1.0 / C, scalar2=None, op0=OP.mult))
            b.w(v, sfx + "V")
            b.inc(sfx + "V", v.tensor_tensor(out=vf32[0:1, 0:TOKCH], in0=srow[0:1, 0:TOKCH],
                                             in1=srow[0:1, 0:TOKCH], op=OP.mult))
            b.w(v, sfx + "V")
            b.inc(sfx + "V", v.tensor_sub(out=qrow[0:1, 0:TOKCH], in0=qrow[0:1, 0:TOKCH],
                                          in1=vf32[0:1, 0:TOKCH]))
        b.V(vS)

        def sS(s):
            b.w(s, sfx + "V")
            b.inc(sfx + "S", s.activation(out=qrow[0:1, 0:TOKCH], in_=qrow[0:1, 0:TOKCH],
                                          func=AF.Ln, bias=epscol[0:1, 0:1]))
            b.w(s, sfx + "S")
            b.inc(sfx + "S", s.activation(out=qrow[0:1, 0:TOKCH], in_=qrow[0:1, 0:TOKCH],
                                          func=AF.Exp, scale=nhalfcol[0:1, 0:1]))
        b.S(sS)

        def tB(t):
            b.w(t, sfx + "S", B0[sfx + "S"] + 2)
            t.matmul(PS_F[:, 0:512], onerow[0:1, :], srow[0:1, 0:512], start=True, stop=True)
            t.matmul(PS_F[:, 512:512 + SCH[1][1]], onerow[0:1, :],
                     srow[0:1, 512:TOKCH], start=True, stop=True)
            t.matmul(PS_A[:, 0:512], onerow[0:1, :], qrow[0:1, 0:512], start=True, stop=True)
            b.inc(sfx + "P", t.matmul(PS_B[:, 0:SCH[1][1]], onerow[0:1, :],
                                      qrow[0:1, 512:TOKCH], start=True, stop=True))
        b.T(tB)

        def vB(v):
            b.w(v, sfx + "P", B0[sfx + "P"] + NCI + 1)
            v.tensor_copy(out=mu_bc[:, 0:512], in_=PS_F[:, 0:512])
            v.tensor_copy(out=mu_bc[:, 512:TOKCH], in_=PS_F[:, 512:512 + SCH[1][1]])
            v.tensor_copy(out=r_bc[:, 0:512], in_=PS_A[:, 0:512])
            b.inc(sfx + "V", v.tensor_copy(out=r_bc[:, 512:TOKCH], in_=PS_B[:, 0:SCH[1][1]]))
        b.V(vB)

    CC_OWN = [(HALO, TOKCH, 0), (0, TOKCH, TOKCH - HALO), (0, TOKCH, 2 * TOKCH - HALO)]

    CHNAMES = ["l2c", "l2q", "l2P", "l2V", "l2S", "wvV",
               "l3q", "l3P", "l3V", "l3S", "xtV", "mxk", "mxv", "mxr",
               "pmk", "pmv", "pmr", "pek", "pev", "per",
               "wpk", "wpv", "wpr", "wpo", "sc", "ym", "ye"]
    for cc in range(NCH):
        c0 = cc * TOKCH
        z = ""
        z2 = ""
        for _n in CHNAMES:
            b.sem(_n)
        B0 = {n: b.cnt[n] for n in CHNAMES}
        def W(eng, n, val, B0=B0):
            b.w(eng, n, B0[n] + val)

        def q_l2(q, c0=c0, z=z):
            if cc == 0:
                b.dwait_all(q, "X1S", NCI)
            for ci in range(NCI):
                if ci >= 2:
                    W(q, "l2c", ci - 1)
                b.dinc("l2l" + z2, cc * NCI + ci, q.dma_start(
                    out=st_in[ci % 2][:, 0:TOKCH],
                    in_=x1scr[128 * ci:128 * (ci + 1), c0:c0 + TOKCH]))
        b.Q(q_l2)

        def v_l2(v, z=z):
            for ci in range(NCI):
                b.dwait(v, "l2l", cc * NCI + ci)
                if ci >= 2:
                    W(v, "l2P", ci - 1)
                b.inc("l2c", v.tensor_copy(out=st_bf[ci % 2][:, 0:TOKCH],
                                               in_=st_in[ci % 2][:, 0:TOKCH]))
        b.V(v_l2)

        def s_l2(s, z=z):
            for ci in range(NCI):
                W(s, "l2c", ci + 1)
                if ci >= 2:
                    W(s, "l2P", ci - 1)
                b.inc("l2q", s.activation(out=st_sq[ci % 2][:, 0:TOKCH],
                                              in_=st_bf[ci % 2][:, 0:TOKCH], func=AF.Square))
        b.S(s_l2)

        stats_from(b, "l2", "l2q", 1,
                   lambda ci: st_bf[ci % 2], lambda ci: st_sq[ci % 2],
                   b.cnt["l2q"] - NCI)

        def q_wv(q, c0=c0, z=z):
            for ci in range(NCI + 1):
                if ci < NCI:
                    if ci >= 2:
                        W(q, "wvV", 4 * (ci - 1))
                    else:
                        W(q, "l2c", NCI)
                    b.dinc("wvL" + z2, cc * NCI + ci, q.dma_start(
                        out=st_in[ci % 2][:, 0:TOKCH],
                        in_=x1scr[128 * ci:128 * (ci + 1), c0:c0 + TOKCH]))
                if ci >= 1:
                    W(q, "wvV", 4 * ci)
                    b.dinc("wvS" + z2, cc * NCI + ci - 1, q.dma_start(
                        out=wspill[128 * (ci - 1):128 * ci, c0:c0 + TOKCH],
                        in_=wtile[(ci - 1) % 2][:, 0:TOKCH]))
        b.Q(q_wv)

        def v_wv(v, z=z):
            b.w(v, "l2V")
            for ci in range(NCI):
                b.dwait(v, "wvL" + z2, cc * NCI + ci)
                b.w(v, "wvV")
                b.inc("wvV", v.tensor_sub(out=vf32[:, 0:TOKCH],
                                          in0=st_in[ci % 2][:, 0:TOKCH],
                                          in1=mu_bc[:, 0:TOKCH]))
                b.w(v, "wvV")
                b.inc("wvV", v.tensor_mul(out=vf32[:, 0:TOKCH], in0=vf32[:, 0:TOKCH],
                                          in1=r_bc[:, 0:TOKCH]))
                b.w(v, "wvV")
                if ci >= 2:
                    b.dwait(v, "wvS" + z2, cc * NCI + ci - 2)
                b.inc("wvV", v.tensor_add(out=wtile[ci % 2][:, 0:TOKCH],
                                          in0=st_in[ci % 2][:, 0:TOKCH],
                                          in1=vf32[:, 0:TOKCH]))
                b.inc("wvV", v.tensor_copy(out=vb[ci][:, 0:TOKCH],
                                           in_=vf32[:, 0:TOKCH]))
        b.V(v_wv)

        def s_l3(s, z=z):
            for ci in range(NCI):
                W(s, "wvV", 4 * (ci + 1))
                if ci >= 2:
                    W(s, "l3P", ci - 1)
                b.inc("l3q", s.activation(out=st_sq[ci % 2][:, 0:TOKCH],
                                              in_=vb[ci][:, 0:TOKCH], func=AF.Square))
        b.S(s_l3)

        stats_from(b, "l3", "l3q", 1,
                   lambda ci: vb[ci], lambda ci: st_sq[ci % 2],
                   b.cnt["l3q"] - NCI)

        def v_xt(v, cc=cc, z=z):
            b.w(v, "l3V")
            for ci in range(NCI):
                b.w(v, "xtV")
                b.inc("xtV", v.tensor_sub(out=vf32[:, 0:TOKCH],
                                              in0=vb[ci][:, 0:TOKCH],
                                              in1=mu_bc[:, 0:TOKCH]))
                b.w(v, "xtV")
                b.inc("xtV", v.tensor_mul(out=vf32[:, 0:TOKCH],
                                              in0=vf32[:, 0:TOKCH],
                                              in1=r_bc[:, 0:TOKCH]))
                b.w(v, "xtV")
                b.inc("xtV", v.tensor_scalar(out=xt[ci][:, 0:TOKCH],
                                                 in0=vf32[:, 0:TOKCH],
                                                 scalar1=lnw[:, ci:ci + 1],
                                                 scalar2=lnb[:, ci:ci + 1],
                                                 op0=OP.mult, op1=OP.add))
            if cc == 0:
                b.w(v, "MKV", 1)
                for ci in range(NCI):
                    b.w(v, "xtV")
                    b.inc("xtV", v.tensor_tensor(out=xt[ci][:, 0:HALO],
                                                     in0=xt[ci][:, 0:HALO],
                                                     in1=maskbc[:, 0:HALO], op=OP.mult))
            bnd_save = [bndA, bndB][(cc + 1) % 2]
            for ci in range(NCI):
                b.w(v, "xtV")
                b.inc("xtV", v.tensor_copy(out=bnd_save[:, ci:ci + 1],
                                               in_=xt[ci][:, TOKCH - 1:TOKCH]))
        b.V(v_xt)

        WSRC = [("k", wkb), ("v", wvb), ("r", wrb), ("o", wob)]

        def q_wp(q, cc=cc, z=z):
            for wi, (nm, src) in enumerate(WSRC):
                if wi >= 2:
                    W(q, "pm" + WSRC[wi - 2][0], 16)
                for ci in range(NCI):
                    b.inc("wp" + nm, q.dma_start(
                        out=wproj[wi % 2][:, 1024 * ci:1024 * (ci + 1)],
                        in_=src[128 * ci:128 * (ci + 1), :]), 16)
        b.Q(q_wp)

        bnd_prev = [bndA, bndB][cc % 2]
        NOFF = [(0, 512), (512, TOKCH - 512)]
        for wi, (pn, dst) in enumerate([("k", K), ("v", Vt), ("r", R)]):
            def v_mx(v, pn=pn, z=z):
                b.w(v, "xtV")
                if pn != "k":
                    W(v, "pm" + {"v": "k", "r": "v"}[pn], 16)
                for ci in range(NCI):
                    b.inc("mx" + pn, v.tensor_scalar(
                        out=mx[ci][:, 0:TOKCH], in0=xt[ci][:, 0:TOKCH],
                        scalar1=mixs[pn][:, ci:ci + 1], scalar2=None, op0=OP.mult))
                    b.w(v, "mx" + pn)
                    b.inc("mx" + pn, v.scalar_tensor_tensor(
                        out=mx[ci][:, 1:TOKCH], in0=xt[ci][:, 0:TOKCH - 1],
                        scalar=mix1m[pn][:, ci:ci + 1], in1=mx[ci][:, 1:TOKCH],
                        op0=OP.mult, op1=OP.add))
                    b.w(v, "mx" + pn)
                    b.inc("mx" + pn, v.scalar_tensor_tensor(
                        out=mx[ci][:, 0:1], in0=bnd_prev[:, ci:ci + 1],
                        scalar=mix1m[pn][:, ci:ci + 1], in1=mx[ci][:, 0:1],
                        op0=OP.mult, op1=OP.add))
            b.V(v_mx)

            def t_pj(t, pn=pn, wi=wi, z=z):
                W(t, "mx" + pn, 3 * NCI)
                W(t, "wp" + pn, 128)
                gi = 0
                for do in range(NCI):
                    for noff, ncw in NOFF:
                        ps = [PS_C, PS_D][gi % 2]
                        if gi >= 2:
                            W(t, "pe" + pn, gi - 1)
                        for ci in range(NCI):
                            ii = t.matmul(ps[:, 0:ncw],
                                          wproj[wi % 2][:, 1024 * ci + 128 * do:
                                                        1024 * ci + 128 * do + 128],
                                          mx[ci][:, noff:noff + ncw],
                                          start=(ci == 0), stop=(ci == NCI - 1),
                                          skip_group_check=True)
                        b.inc("pm" + pn, ii)
                        gi += 1
            b.T(t_pj)

            if pn == "r":
                def s_ev(s, pn=pn, dst=dst, z=z):
                    gi = 0
                    for do in range(NCI):
                        for noff, ncw in NOFF:
                            ps = [PS_C, PS_D][gi % 2]
                            W(s, "pm" + pn, gi + 1)
                            b.inc("pe" + pn, s.activation(
                                out=dst[do][:, noff:noff + ncw], in_=ps[:, 0:ncw],
                                func=AF.Sigmoid))
                            gi += 1
                b.S(s_ev)
            else:
                def v_ev(v, pn=pn, dst=dst, z=z):
                    gi = 0
                    for do in range(NCI):
                        for noff, ncw in NOFF:
                            ps = [PS_C, PS_D][gi % 2]
                            W(v, "pm" + pn, gi + 1)
                            b.inc("pe" + pn, v.tensor_copy(
                                out=dst[do][:, noff:noff + ncw], in_=ps[:, 0:ncw]))
                            gi += 1
                b.V(v_ev)

        if DBG and cc == 0:
            def g_dbg(g, z=z):
                W(g, "pek", 16)
                W(g, "pev", 16)
                W(g, "per", 16)
                for ci in range(NCI):
                    b.inc("DBG2", g.dma_start(out=o_dk[128 * ci:128 * (ci + 1), :],
                                              in_=K[ci][:, 0:TOKCH]), 16)
                    b.inc("DBG2", g.dma_start(out=o_dv[128 * ci:128 * (ci + 1), :],
                                              in_=Vt[ci][:, 0:TOKCH]), 16)
                    b.inc("DBG2", g.dma_start(out=o_dr[128 * ci:128 * (ci + 1), :],
                                              in_=R[ci][:, 0:TOKCH]), 16)
                    b.inc("DBG2", g.dma_start(out=o_dxt[128 * ci:128 * (ci + 1), :],
                                              in_=xt[ci][:, 0:TOKCH]), 16)
            b.G(g_dbg)

            def q_dbgw(q, c0=c0, z=z):
                for ci in range(NCI):
                    b.dwait(q, "wvS", ci)
                    b.inc("DBG1", q.dma_start(out=o_dw[128 * ci:128 * (ci + 1), c0:c0 + TOKCH],
                                              in_=wspill[128 * ci:128 * (ci + 1), c0:c0 + TOKCH]), 16)
            b.Q(q_dbgw)

        def v_scan(v, cc=cc, z=z):
            W(v, "pek", 16)
            W(v, "pev", 16)
            W(v, "per", 16)
            if DBG and cc == 0:
                b.w(v, "DBG2")
            b.w(v, "INI2", 1)
            for ci in range(NCI):
                ini = 0.0 if cc == 0 else carry[:, ci:ci + 1]
                b.inc("sc", v.tensor_tensor_scan(
                    out=Ssc[:, 0:TOKCH],
                    data0=dk[:, ci:ci + 1].to_broadcast([128, TOKCH]),
                    data1=K[ci][:, 0:TOKCH],
                    initial=ini, op0=OP.mult, op1=OP.add))
                b.w(v, "sc")
                b.inc("sc", v.tensor_copy(out=carry[:, ci:ci + 1],
                                              in_=Ssc[:, TOKCH - 1:TOKCH]))
                b.inc("sc", v.scalar_tensor_tensor(
                    out=K[ci][:, 0:TOKCH], in0=Ssc[:, 0:TOKCH],
                    scalar=tf[:, ci:ci + 1], in1=K[ci][:, 0:TOKCH],
                    op0=OP.mult, op1=OP.add))
                b.w(v, "sc")
                b.inc("sc", v.tensor_mul(out=Vt[ci][:, 0:TOKCH],
                                             in0=K[ci][:, 0:TOKCH],
                                             in1=Vt[ci][:, 0:TOKCH]))
                b.w(v, "sc")
                b.inc("sc", v.tensor_mul(out=R[ci][:, 0:TOKCH],
                                             in0=Vt[ci][:, 0:TOKCH],
                                             in1=R[ci][:, 0:TOKCH]))
        b.V(v_scan)

        def q_wl(q, c0=c0, cc=cc, z=z):
            lo, hi, ooff = CC_OWN[cc]
            for do in range(NCI + 1):
                if do < NCI:
                    if do >= 2:
                        W(q, "ye", 2 * (do - 1))
                    else:
                        b.dwait_all(q, "wvS" + z2, (cc + 1) * NCI)
                    b.dinc("wl" + z2, cc * NCI + do, q.dma_start(
                        out=wtile[do % 2][:, 0:TOKCH],
                        in_=wspill[128 * do:128 * (do + 1), c0:c0 + TOKCH]))
                if do >= 1:
                    W(q, "ye", 2 * do)
                    b.dinc("x2s" + z2, cc * NCI + do - 1, q.dma_start(
                        out=o_x2[128 * (do - 1):128 * do, ooff:ooff + (hi - lo)],
                        in_=x2t[(do - 1) % 2][:, lo:hi]))
        b.Q(q_wl)

        def t_y(t, z=z):
            W(t, "sc", 5 * NCI)
            W(t, "wpo", 128)
            gi = 0
            for do in range(NCI):
                for noff, ncw in NOFF:
                    ps = [PS_C, PS_D][gi % 2]
                    if gi >= 2:
                        W(t, "ye", gi - 1)
                    for ci in range(NCI):
                        ii = t.matmul(ps[:, 0:ncw],
                                      wproj[1][:, 1024 * ci + 128 * do:
                                                1024 * ci + 128 * do + 128],
                                      R[ci][:, noff:noff + ncw],
                                      start=(ci == 0), stop=(ci == NCI - 1),
                                      skip_group_check=True)
                    b.inc("ym", ii)
                    gi += 1
        b.T(t_y)

        def v_y(v, cc=cc, z=z):
            lo, hi, ooff = CC_OWN[cc]
            gi = 0
            for do in range(NCI):
                b.dwait(v, "wl" + z2, cc * NCI + do)
                if cc * NCI + do >= 2:
                    b.dwait(v, "x2s" + z2, cc * NCI + do - 2)
                for noff, ncw in NOFF:
                    ps = [PS_C, PS_D][gi % 2]
                    W(v, "ym", gi + 1)
                    b.inc("ye", v.tensor_tensor(
                        out=x2t[do % 2][:, noff:noff + ncw], in0=ps[:, 0:ncw],
                        in1=wtile[do % 2][:, noff:noff + ncw], op=OP.add))
                    gi += 1
        b.V(v_y)


    b.pop()

    if stage <= 3:
        def g_fin3(g):
            b.dwait_all(g, "x2s", NCH * NCI)
        b.G(g_fin3)
        return b

    xs2 = [b.alloc(TOWN, F32) for _ in range(NCI)]
    w9s = b.alloc(9 * NCI, F32)
    prob2 = b.alloc((TOWN // 128) * 8, F32)
    murow = b.alloc(TOWN, F32)
    rrow = b.alloc(TOWN, F32)
    mu_b4 = b.alloc(TOWN, F32)
    r_b4 = b.alloc(TOWN, F32)

    def q_l4(q):
        b.dwait_all(q, "x2s", NCH * NCI)
        for ci in range(NCI):
            b.inc("L4L", q.dma_start(out=xs2[ci][:], in_=o_x2[128 * ci:128 * (ci + 1), :]), 16)
        for ci in range(NCI):
            b.inc("L4L", q.dma_start(out=w9s[:, 9 * ci:9 * ci + 9],
                                     in_=w9[128 * ci:128 * (ci + 1), :]), 16)
    b.Q(q_l4)
    l4l = b.cnt["L4L"]

    def pre4(eng):
        b.w(eng, "L4L", l4l)
    for sec in (b.V, b.S, b.T):
        sec(pre4)

    emit_ln_gate(b, xs2, TOWN, idf, w9s, onescol, onerow, epscol, nhalfcol,
                 PS_A, PS_B, PS_E, o_xn2, prob2, mrow, murow, rrow, mu_b4, r_b4, "G4")

    def q_out4(q):
        b.w(q, "G4VEc")
        b.inc("ST4", q.dma_start(out=o_pr2, in_=prob2[:, :]), 16)
    b.Q(q_out4)

    def g_fin(g):
        if DBG:
            b.w(g, "DBG1")
            b.w(g, "DBG2")
        b.w(g, "ST4")
        b.dwait_all(g, "G4XNS", NCI)
        b.dwait_all(g, "x2s", NCH * NCI)
    b.G(g_fin)
    return b



def build_ffn2():
    b = B(176 * 1024)
    nc = b.nc
    ntj = TOWN // 128
    x2h = nc.dram_tensor("x2h", [C, TOWN], F32, kind="ExternalInput").ap()
    xn2h = nc.dram_tensor("xn2h", [C, TOWN], BF16, kind="ExternalInput").ap()
    pr2 = nc.dram_tensor("pr2", [128, ntj * 8], F32, kind="ExternalInput").ap()
    tau2 = nc.dram_tensor("tau2", [1, 8], F32, kind="ExternalInput").ap()
    ident = nc.dram_tensor("ident", [128, 128], F32, kind="ExternalInput").ap()
    ut = nc.dram_tensor("ut", [128, 128], F32, kind="ExternalInput").ap()
    iota8 = nc.dram_tensor("iota8", [1, 8], F32, kind="ExternalInput").ap()
    iotaB = nc.dram_tensor("iotaB", [1, 8], F32, kind="ExternalInput").ap()
    w1b = nc.dram_tensor("w1b", [8 * C, FF], BF16, kind="ExternalInput").ap()
    w2b = nc.dram_tensor("w2b", [8 * FF, C], BF16, kind="ExternalInput").ap()
    o_x3 = nc.dram_tensor("o_x3", [C, TOWN], F32, kind="ExternalOutput").ap()
    frame = nc.dram_tensor("frame", [SLOTS, C], BF16, kind="Internal").ap()
    retf = nc.dram_tensor("retf", [SLOTS, C], BF16, kind="Internal").ap()
    tmscr = nc.dram_tensor("tmscr", [TOWN, C], BF16, kind="Internal").ap()

    b.start()
    PS_A = b.ctx.enter_context(nc.psum_tensor("PS_A", [128, 512], F32))
    PS_B = b.ctx.enter_context(nc.psum_tensor("PS_B", [128, 512], F32))
    PS_C = b.ctx.enter_context(nc.psum_tensor("PS_C", [128, 512], F32))
    PS_D = b.ctx.enter_context(nc.psum_tensor("PS_D", [128, 512], F32))
    PS_E = b.ctx.enter_context(nc.psum_tensor("PS_E", [128, 1024], F32))
    PS_F = b.ctx.enter_context(nc.psum_tensor("PS_F", [128, 1024], F32))

    prob = b.alloc(ntj * 8, F32)
    gate = b.alloc(ntj, F32)
    idx = b.alloc(ntj, I32)
    trow = b.alloc(24, F32)
    taur = trow[:, 0:8]
    io8r = trow[:, 8:16]
    ioBr = trow[:, 16:24]
    onescol = b.alloc(1, F32)
    onerow = b.alloc(128, F32)
    idf = b.alloc(128, F32)
    utsb = b.alloc(128, F32)

    def q_init(q):
        b.inc("LD", q.dma_start(out=prob[:, :], in_=pr2), 16)
        b.inc("LD", q.dma_start(out=taur[0:1, 0:8], in_=tau2), 16)
        b.inc("LD", q.dma_start(out=io8r[0:1, 0:8], in_=iota8), 16)
        b.inc("LD", q.dma_start(out=ioBr[0:1, 0:8], in_=iotaB), 16)
        b.inc("LD", q.dma_start(out=idf[:], in_=ident), 16)
        b.inc("LD", q.dma_start(out=utsb[:], in_=ut), 16)
    b.Q(q_init)
    ld0 = b.cnt["LD"]

    def v_init(v):
        b.w(v, "LD", ld0)
        v.memset(onescol[:, 0:1], 1.0)
        b.inc("INI", v.memset(onerow[0:1, :], 1.0))
    b.V(v_init)

    def pre_r(eng):
        b.w(eng, "LD", ld0)
        b.w(eng, "INI", 1)
    for sec in (b.V, b.S, b.T, b.G, b.Q):
        sec(pre_r)

    emit_routing(b, ntj, prob, trow, io8r, ioBr, onescol, onerow, utsb,
                 PS_C, PS_D, PS_A, gate, idx, "r")
    b.push()
    emit_ffn(b, ntj, xn2h, idx, w1b, w2b, frame, retf,
             idf, PS_A, PS_B, PS_E, PS_F, "f")
    b.pop()

    gtm = [b.ctx.enter_context(b.nc.sbuf_tensor("gtm%d" % i, [128, 1024], BF16)).ap()
           for i in range(2)]
    xht = [b.alloc(TOWN, F32) for _ in range(2)]
    ffnt = [b.alloc(TOWN, BF16) for _ in range(2)]
    x3t = [b.alloc(TOWN, F32) for _ in range(2)]

    gts = [b.alloc(1024, BF16) for _ in range(2)]

    def g_gb(g):
        b.w(g, "fRSTA")
        b.w(g, "fRSTB")
        for tj in range(ntj):
            if tj >= 2:
                b.w(g, "GBV", tj - 1)
            b.dinc("GBK", tj, g.indirect_dma_start(
                out=gtm[tj % 2][:], out_offset=None,
                in_=retf, in_offset=bass.IndirectOffsetOnAxis(idx[:, tj:tj + 1], 0),
                bounds_check=SLOTS - 1, oob_is_err=False))
    b.G(g_gb)

    def v_gb(v):
        for tj in range(ntj):
            b.dwait(v, "GBK", tj)
            if tj >= 2:
                b.dwait(v, "GBS", tj - 2)
            b.inc("GBV", v.tensor_scalar(out=gts[tj % 2][:], in0=gtm[tj % 2][:],
                                         scalar1=gate[:, tj:tj + 1], scalar2=None,
                                         op0=OP.mult))
    b.V(v_gb)

    def q_gb2(q):
        for tj in range(ntj):
            b.w(q, "GBV", tj + 1)
            b.dinc("GBS", tj, q.dma_start(out=tmscr[128 * tj:128 * (tj + 1), :],
                                          in_=gts[tj % 2][:]))
    b.Q(q_gb2)

    def q_x3(q):
        b.dwait_all(q, "GBS", ntj)
        for ci in range(NCI + 1):
            if ci < NCI:
                if ci >= 2:
                    b.w(q, "X3V", ci - 1)
                q.dma_start(out=ffnt[ci % 2][:],
                            in_=tmscr[:, 128 * ci:128 * (ci + 1)],
                            transpose=True).then_inc(b.sem("X3T" + "AB"[ci % 2]), 16)
                b.dinc("X3T", ci, q.dma_start(out=xht[ci % 2][:],
                                              in_=x2h[128 * ci:128 * (ci + 1), :]))
                b.cnt["X3T" + "AB"[ci % 2]] += 16
            if ci >= 1:
                b.w(q, "X3V", ci)
                b.dinc("X3S", ci - 1, q.dma_start(
                    out=o_x3[128 * (ci - 1):128 * ci, :], in_=x3t[(ci - 1) % 2][:]))
    b.Q(q_x3)

    def v_x3(v):
        for ci in range(NCI):
            b.dwait(v, "X3T", ci, per=32)
            if ci >= 2:
                b.dwait(v, "X3S", ci - 2)
            b.inc("X3V", v.tensor_tensor(out=x3t[ci % 2][:], in0=xht[ci % 2][:],
                                         in1=ffnt[ci % 2][:], op=OP.add))
    b.V(v_x3)

    def g_fin(g):
        b.dwait_all(g, "X3S", NCI)
    b.G(g_fin)
    return b



_PROGS = {}
DEBUG = {}


def _progs():
    if "b1" not in _PROGS:
        _PROGS["b1"] = build_gate1()
        _PROGS["b2"] = build_tau()
        _PROGS["b3"] = build_main()
        _PROGS["b5"] = build_ffn2()
    return _PROGS


def kernel(L0_attn_x, L0_attn_time_mix_k, L0_attn_time_mix_v, L0_attn_time_mix_r,
           L0_attn_time_decay, L0_attn_time_first, L0_attn_Wk, L0_attn_Wv,
           L0_attn_Wr, L0_attn_Wo, L0_attn_ln_w, L0_attn_ln_b, L0_ffn_Wg,
           L0_ffn_W1, L0_ffn_W2):
    P = _progs()
    cores = list(range(NCORES))
    f32 = np.float32
    x = np.asarray(L0_attn_x, f32)[0]
    xT = np.ascontiguousarray(x.T)
    Wg = np.asarray(L0_ffn_Wg, f32)
    W1 = np.asarray(L0_ffn_W1, f32)
    W2 = np.asarray(L0_ffn_W2, f32)
    w9 = np.ascontiguousarray(np.concatenate([Wg, np.ones((C, 1), f32)], 1))
    ident = np.eye(128, dtype=f32)
    ut = np.triu(np.ones((128, 128), f32), 1)
    iota8 = np.arange(8, dtype=f32).reshape(1, 8)
    iotaB = (np.arange(8, dtype=f32) + 1024.0).reshape(1, 8)
    grid0 = np.arange(1, TAU_W + 1, dtype=f32).reshape(1, TAU_W)
    attnW = [np.ascontiguousarray(np.asarray(w, f32).T)
             for w in (L0_attn_Wk, L0_attn_Wv, L0_attn_Wr, L0_attn_Wo)]

    ins1 = []
    for c in cores:
        lo = TOWN * c - HALO
        xh = np.zeros((C, TT), f32)
        s0 = max(lo, 0)
        xh[:, s0 - lo:] = xT[:, s0:TOWN * (c + 1)]
        wc = np.ascontiguousarray(np.concatenate(
            [W1[c], W2[c].reshape(C, FF), attnW[c % 4].reshape(512, FF)], 0))
        ins1.append({"xh": xh, "w9": w9, "ident": ident, "wc_in": wc})
    r1 = run_bass_kernel_spmd(P["b1"].nc, ins1, core_ids=cores).results
    probs_full = np.concatenate(
        [r1[c]["o_pr"].reshape(128, NTJ, 8).transpose(1, 0, 2).reshape(TT, 8)[HALO:]
         for c in cores], 0)
    w1b = np.concatenate([r1[c]["o_wc"][0:C] for c in cores], 0)
    w2b = np.concatenate([r1[c]["o_wc"][C:2 * C].reshape(FF, C) for c in cores], 0)
    wkb, wvb, wrb, wob = (r1[c]["o_wc"][2 * C:].reshape(C, C) for c in range(4))
    DEBUG["probs1"] = probs_full

    ins2 = [{"pp": np.ascontiguousarray(probs_full[:, [c] + [e for e in range(8) if e != c]]),
             "grid0": grid0} for c in cores]
    r2 = run_bass_kernel_spmd(P["b2"].nc, ins2, core_ids=cores).results
    tau1 = np.array([[float(r2[c]["o_tau"][0, 0]) for c in cores]], f32)
    DEBUG["tau1"] = tau1

    def cols(vec):
        return np.ascontiguousarray(np.asarray(vec, f32).reshape(8, 128).T)
    mixk, mixv, mixr = cols(L0_attn_time_mix_k), cols(L0_attn_time_mix_v), cols(L0_attn_time_mix_r)
    tdk, tfk = cols(L0_attn_time_decay), cols(L0_attn_time_first)
    lnwk, lnbk = cols(L0_attn_ln_w), cols(L0_attn_ln_b)
    ins3 = []
    for c in cores:
        hm = np.ones((1, HALO), f32)
        if c == 0:
            hm[:] = 0.0
        ins3.append({
            "xh": ins1[c]["xh"], "xn1h": r1[c]["o_xn"], "pr1": r1[c]["o_pr"],
            "tau1": tau1, "w9": w9, "ident": ident, "ut": ut,
            "iota8": iota8, "iotaB": iotaB, "hmask": hm,
            "mixk": mixk, "mixv": mixv, "mixr": mixr, "tdk": tdk, "tfk": tfk,
            "lnwk": lnwk, "lnbk": lnbk, "w1b": w1b, "w2b": w2b,
            "wkb": wkb, "wvb": wvb, "wrb": wrb, "wob": wob,
        })
    r3 = run_bass_kernel_spmd(P["b3"].nc, ins3, core_ids=cores).results
    for dk_ in ("o_dx1", "o_dw", "o_dk", "o_dv", "o_dr", "o_dxt", "o_didx", "o_dgate", "o_rtaub", "o_riotab", "o_riotBb", "o_rt2e", "o_rasn", "o_rcrank", "o_roffb"):
        if dk_ in r3[0]:
            DEBUG[dk_] = [r3[c][dk_] for c in cores]
    probs2_full = np.concatenate(
        [r3[c]["o_pr2"].reshape(128, TOWN // 128, 8).transpose(1, 0, 2).reshape(TOWN, 8)
         for c in cores], 0)
    DEBUG["x2"] = np.concatenate([r3[c]["o_x2"].T for c in cores], 0)
    DEBUG["probs2"] = probs2_full

    ins4 = [{"pp": np.ascontiguousarray(probs2_full[:, [c] + [e for e in range(8) if e != c]]),
             "grid0": grid0} for c in cores]
    r4 = run_bass_kernel_spmd(P["b2"].nc, ins4, core_ids=cores).results
    tau2 = np.array([[float(r4[c]["o_tau"][0, 0]) for c in cores]], f32)
    DEBUG["tau2"] = tau2

    ins5 = []
    for c in cores:
        ins5.append({
            "x2h": r3[c]["o_x2"], "xn2h": r3[c]["o_xn2"], "pr2": r3[c]["o_pr2"],
            "tau2": tau2, "ident": ident, "ut": ut, "iota8": iota8, "iotaB": iotaB,
            "w1b": w1b, "w2b": w2b,
        })
    r5 = run_bass_kernel_spmd(P["b5"].nc, ins5, core_ids=cores).results
    out = np.concatenate([r5[c]["o_x3"].T for c in cores], 0)
    return np.ascontiguousarray(out).reshape(1, 16384, C)


# version 07 (final)
# speedup vs baseline: 1.0305x; 1.0305x over previous
import numpy as np
import concourse.bass as bass
import concourse.mybir as mybir
from concourse.bass_utils import run_bass_kernel_spmd
from contextlib import ExitStack

F32 = mybir.dt.float32
BF16 = mybir.dt.bfloat16
I32 = mybir.dt.int32
U8 = mybir.dt.uint8
AF = mybir.ActivationFunctionType
OP = mybir.AluOpType
AX = mybir.AxisListType

NCORES = 8
C = 1024
NCI = 8
TOWN = 2048
HALO = 256
TT = TOWN + HALO
NTJ = TT // 128
E = 8
FF = 2048
NFI = FF // 128
CAP = 384
SLOTS = E * CAP
NSJ = SLOTS // 128
LNEPS = 1e-5
BIG = 1.0e9
TAU_ITERS = 11
TAU_W = 8


class B:
    def __init__(self, arena_bytes):
        self.nc = bass.Bass("TRN2", target_bir_lowering=False)
        self.ctx = ExitStack()
        self._arena_bytes = arena_bytes
        self._off = 0
        self._marks = []
        self._sems = {}
        self.cnt = {}

    def start(self):
        self._arena = self.ctx.enter_context(
            self.nc.sbuf_tensor("arena", [128, self._arena_bytes], U8))
        self.block = self.ctx.enter_context(self.nc.Block())

    def alloc(self, cols, dtype):
        nbytes = cols * mybir.dt.size(dtype)
        ap = self._arena.ap()[:, self._off:self._off + nbytes].bitcast(dtype)
        self._off += (nbytes + 63) & ~63
        assert self._off <= self._arena_bytes, f"sbuf arena overflow {self._off}"
        return ap

    def push(self): self._marks.append(self._off)
    def pop(self): self._off = self._marks.pop()

    def sem(self, name):
        if name not in self._sems:
            self._sems[name] = self.ctx.enter_context(self.nc.semaphore(name))
            self.cnt[name] = 0
        return self._sems[name]

    def inc(self, name, instr, k=1):
        instr.then_inc(self.sem(name), k)
        self.cnt[name] += k
        return self.cnt[name]

    def w(self, eng, name, val=None):
        eng.wait_ge(self.sem(name), self.cnt[name] if val is None else val)

    def dinc(self, base, i, instr, per=16):
        return self.inc(base + "AB"[i % 2], instr, per)

    def dwait(self, eng, base, i, per=16):
        eng.wait_ge(self.sem(base + "AB"[i % 2]), per * (i // 2 + 1))

    def dwait_all(self, eng, base, n, per=16):
        if n <= 0:
            return
        eng.wait_ge(self.sem(base + "A"), per * ((n + 1) // 2))
        if n >= 2:
            eng.wait_ge(self.sem(base + "B"), per * (n // 2))

    def V(self, f): self.block.vector(f)
    def S(self, f): self.block.scalar(f)
    def T(self, f): self.block.tensor(f)
    def G(self, f): self.block.gpsimd(f)
    def Q(self, f): self.block.sync(f)



def emit_ln_gate(b, xs, nt, ident_f32, w9s, onescol, onerow, epscol, nhalfcol,
                 psG, psH, psK, out_xn_hbm, out_prob, mrow, murow, rrow,
                 mu_b, r_b, pfx):
    nc = b.nc
    ntj = nt // 128
    F16 = mybir.dt.float16
    sqh = [b.alloc(nt, F16) for _ in range(NCI)]
    ones16 = b.alloc(1, F16)
    raw = b.alloc(ntj * 10, F32)
    mu_t = b.alloc(ntj, F32)
    r_t = b.alloc(ntj, F32)
    scr_t = b.alloc(ntj, F32)
    csrow = b.alloc(16, F32)
    csb = b.alloc(8, F32)
    lscr = b.alloc(ntj * 8, F32)
    smax = b.alloc(ntj, F32)
    trsc = b.alloc(256, F32)
    xnp = [b.alloc(nt, BF16) for _ in range(2)]
    tsub = b.alloc(nt, F32)

    SQ, PEc, VEc, SEc, LD2, PEd, XNS = (pfx + n for n in
                                        ["SQ", "PEc", "VEc", "SEc", "LD2", "PEd", "XNS"])

    def v0(v):
        b.inc(VEc, v.memset(ones16[:, 0:1], 1.0))
    b.V(v0)
    ve_ones = b.cnt[VEc]

    def s1(s):
        for ci in range(NCI):
            b.inc(SQ, s.activation(out=sqh[ci][:], in_=xs[ci][:], func=AF.Square))
    b.S(s1)

    def t1(t):
        b.w(t, SQ, NCI)
        b.w(t, VEc, ve_ones)
        for tj in range(ntj):
            if tj >= 2:
                b.w(t, VEc, ve_ones + tj - 1)
            bkx = [psG, psH][tj % 2]
            qoff = 512 * (tj % 2)
            for ci in range(NCI):
                t.matmul(bkx[:, 0:9], xs[ci][:, 128 * tj:128 * tj + 128],
                         w9s[:, 9 * ci:9 * ci + 9],
                         start=(ci == 0), stop=(ci == NCI - 1), skip_group_check=True)
            for ci in range(NCI):
                ii = t.matmul(psK[:, qoff:qoff + 1],
                              sqh[ci][:, 128 * tj:128 * tj + 128], ones16[:, 0:1],
                              start=(ci == 0), stop=(ci == NCI - 1),
                              skip_group_check=True)
            b.inc(PEc, ii)
    b.T(t1)

    def v_ev(v):
        for tj in range(ntj):
            b.w(v, PEc, tj + 1)
            bkx = [psG, psH][tj % 2]
            qoff = 512 * (tj % 2)
            v.tensor_copy(out=raw[:, 10 * tj:10 * tj + 9], in_=bkx[:, 0:9])
            b.inc(VEc, v.tensor_copy(out=raw[:, 10 * tj + 9:10 * tj + 10],
                                     in_=psK[:, qoff:qoff + 1]))
    b.V(v_ev)
    ve_raw = b.cnt[VEc]

    def t1b(t):
        b.w(t, VEc, ve_raw)
        for ci in range(NCI):
            last = t.matmul(psH[0:1, 0:8], onescol[:, 0:1], w9s[:, 9 * ci:9 * ci + 8],
                            start=(ci == 0), stop=(ci == NCI - 1), skip_group_check=True)
        b.inc(PEc, last)
    b.T(t1b)

    rawv = raw[:, 0:10 * ntj].rearrange("p (tj k) -> p tj k", k=10)
    INV_C = 1.0 / C

    def v1(v):
        b.w(v, PEc, ntj + 1)
        b.w(v, VEc, ve_raw)
        b.inc(VEc, v.tensor_scalar(out=mu_t[:, 0:ntj], in0=rawv[:, :, 8:9].squeeze(2),
                                   scalar1=INV_C, scalar2=None, op0=OP.mult))
        b.inc(VEc, v.tensor_scalar(out=r_t[:, 0:ntj], in0=rawv[:, :, 9:10].squeeze(2),
                                   scalar1=INV_C, scalar2=None, op0=OP.mult))
        b.w(v, VEc)
        b.inc(VEc, v.tensor_mul(out=scr_t[:, 0:ntj], in0=mu_t[:, 0:ntj], in1=mu_t[:, 0:ntj]))
        b.w(v, VEc)
        b.inc(VEc, v.tensor_sub(out=r_t[:, 0:ntj], in0=r_t[:, 0:ntj], in1=scr_t[:, 0:ntj]))
        b.inc(VEc, v.tensor_copy(out=csrow[0:1, 0:8], in_=psH[0:1, 0:8]))
    b.V(v1)

    def s2(s):
        b.w(s, VEc)
        b.inc(SEc, s.activation(out=r_t[:, 0:ntj], in_=r_t[:, 0:ntj], func=AF.Ln,
                                bias=epscol[:, 0:1]))
        b.w(s, SEc)
        b.inc(SEc, s.activation(out=r_t[:, 0:ntj], in_=r_t[:, 0:ntj], func=AF.Exp,
                                scale=nhalfcol[:, 0:1]))
    b.S(s2)

    def t2(t):
        b.w(t, VEc)
        b.inc(PEd, t.matmul(psH[:, 8:16], onerow[0:1, :], csrow[0:1, 0:8],
                            start=True, stop=True))
        b.w(t, SEc)
        b.inc(PEd, t.transpose(psK[0:ntj, 0:128], mu_t[:, 0:ntj], ident_f32[:]))
        b.inc(PEd, t.transpose(psK[0:ntj, 512:640], r_t[:, 0:ntj], ident_f32[:]))
    b.T(t2)

    def v2(v):
        b.w(v, PEd)
        b.inc(VEc, v.tensor_copy(out=csb[:, 0:8], in_=psH[:, 8:16]))
        b.inc(VEc, v.tensor_copy(out=trsc[0:ntj, 0:128], in_=psK[0:ntj, 0:128]))
        b.inc(VEc, v.tensor_copy(out=trsc[0:ntj, 128:256], in_=psK[0:ntj, 512:640]))
    b.V(v2)

    def q1(q):
        b.w(q, VEc)
        b.inc(LD2, q.dma_start(out=mrow[0:1, 0:nt].rearrange("o (a z) -> (o a) z", z=128),
                               in_=trsc[0:ntj, 0:128]), 16)
        b.inc(LD2, q.dma_start(out=mrow[1:2, 0:nt].rearrange("o (a z) -> (o a) z", z=128),
                               in_=trsc[0:ntj, 128:256]), 16)
        b.w(q, LD2)
        b.inc(LD2, q.dma_start(out=murow[0:1, 0:nt], in_=mrow[0:1, 0:nt]), 16)
        b.inc(LD2, q.dma_start(out=rrow[0:1, 0:nt], in_=mrow[1:2, 0:nt]), 16)
    b.Q(q1)

    def v3(v):
        b.w(v, VEc)
        b.inc(VEc, v.tensor_tensor(out=lscr[:, 0:ntj * 8],
                                   in0=mu_t[:, 0:ntj].unsqueeze(2).to_broadcast([128, ntj, 8]),
                                   in1=csb[:, 0:8].unsqueeze(1).to_broadcast([128, ntj, 8]),
                                   op=OP.mult))
        b.w(v, VEc)
        b.inc(VEc, v.tensor_tensor(out=lscr[:, 0:ntj * 8],
                                   in0=rawv[:, :, 0:8],
                                   in1=lscr[:, 0:ntj * 8].rearrange("p (a z) -> p a z", z=8),
                                   op=OP.subtract))
        b.w(v, VEc)
        b.w(v, SEc)
        b.inc(VEc, v.tensor_tensor(out=lscr[:, 0:ntj * 8],
                                   in0=lscr[:, 0:ntj * 8].rearrange("p (a z) -> p a z", z=8),
                                   in1=r_t[:, 0:ntj].unsqueeze(2).to_broadcast([128, ntj, 8]),
                                   op=OP.mult))
        b.w(v, VEc)
        b.inc(VEc, v.reduce_max(out=smax[:, 0:ntj],
                                in_=lscr[:, 0:ntj * 8].rearrange("p (a z) -> p a z", z=8),
                                axis=AX.X))
        b.w(v, VEc)
        b.inc(VEc, v.tensor_tensor(out=out_prob[:, 0:ntj * 8],
                                   in0=lscr[:, 0:ntj * 8].rearrange("p (a z) -> p a z", z=8),
                                   in1=smax[:, 0:ntj].unsqueeze(2).to_broadcast([128, ntj, 8]),
                                   op=OP.subtract))
    b.V(v3)

    def s3(s):
        b.w(s, VEc)
        b.inc(SEc, s.activation(out=out_prob[:, 0:ntj * 8], in_=out_prob[:, 0:ntj * 8],
                                func=AF.Exp))
    b.S(s3)

    def v4(v):
        b.w(v, SEc)
        b.inc(VEc, v.reduce_sum(out=smax[:, 0:ntj],
                                in_=out_prob[:, 0:ntj * 8].rearrange("p (a z) -> p a z", z=8),
                                axis=AX.X))
        b.w(v, VEc)
        b.inc(VEc, v.reciprocal(out=smax[:, 0:ntj], in_=smax[:, 0:ntj]))
        b.w(v, VEc)
        b.inc(VEc, v.tensor_tensor(out=out_prob[:, 0:ntj * 8],
                                   in0=out_prob[:, 0:ntj * 8].rearrange("p (a z) -> p a z", z=8),
                                   in1=smax[:, 0:ntj].unsqueeze(2).to_broadcast([128, ntj, 8]),
                                   op=OP.mult))
    b.V(v4)

    chunks = [(k, min(512, nt - k)) for k in range(0, nt, 512)]
    ve_base = b.cnt[VEc]
    pe_base = b.cnt[PEd]

    def t3(t):
        b.w(t, LD2)
        for i, (off, cw) in enumerate(chunks):
            if i >= 1:
                b.w(t, VEc, ve_base + i)
            t.matmul(psK[:, 0:cw], onerow[0:1, :], murow[0:1, off:off + cw],
                     start=True, stop=True)
            b.inc(PEd, t.matmul(psK[:, 512:512 + cw], onerow[0:1, :],
                                rrow[0:1, off:off + cw], start=True, stop=True))
    b.T(t3)

    def v5(v):
        for i, (off, cw) in enumerate(chunks):
            b.w(v, PEd, pe_base + i + 1)
            v.tensor_copy(out=mu_b[:, off:off + cw], in_=psK[:, 0:cw])
            b.inc(VEc, v.tensor_copy(out=r_b[:, off:off + cw], in_=psK[:, 512:512 + cw]))
    b.V(v5)

    def v6(v):
        b.w(v, VEc)
        for ci in range(NCI):
            if ci >= 2:
                b.dwait(v, XNS, ci - 2)
            b.w(v, VEc)
            b.inc(VEc, v.tensor_sub(out=tsub[:], in0=xs[ci][:], in1=mu_b[:, 0:nt]))
            b.w(v, VEc)
            b.inc(VEc, v.tensor_mul(out=xnp[ci % 2][:], in0=tsub[:], in1=r_b[:, 0:nt]))
    b.V(v6)
    ve_n = b.cnt[VEc]

    def q2(q):
        for ci in range(NCI):
            b.w(q, VEc, ve_n - 2 * (NCI - ci) + 2)
            b.dinc(XNS, ci, q.dma_start(out=out_xn_hbm[128 * ci:128 * (ci + 1), :],
                                        in_=xnp[ci % 2][:]))
    b.Q(q2)



def build_gate1():
    b = B(176 * 1024)
    nc = b.nc
    xh = nc.dram_tensor("xh", [C, TT], F32, kind="ExternalInput").ap()
    w9 = nc.dram_tensor("w9", [C, 9], F32, kind="ExternalInput").ap()
    ident = nc.dram_tensor("ident", [128, 128], F32, kind="ExternalInput").ap()
    o_xn = nc.dram_tensor("o_xn", [C, TT], BF16, kind="ExternalOutput").ap()
    o_pr = nc.dram_tensor("o_pr", [128, NTJ * 8], F32, kind="ExternalOutput").ap()
    mrow = nc.dram_tensor("mrow", [2, TT], F32, kind="Internal").ap()

    b.start()
    xs = [b.alloc(TT, F32) for _ in range(NCI)]
    w9s = b.alloc(9 * NCI, F32)
    onescol = b.alloc(1, F32)
    onerow = b.alloc(128, F32)
    epscol = b.alloc(1, F32)
    nhalfcol = b.alloc(1, F32)
    idf = b.alloc(128, F32)
    prob = b.alloc(NTJ * 8, F32)
    murow = b.alloc(TT, F32)
    rrow = b.alloc(TT, F32)
    mu_b = b.alloc(TT, F32)
    r_b = b.alloc(TT, F32)

    psG = b.ctx.enter_context(nc.psum_tensor("psG", [128, 512], F32))
    psH = b.ctx.enter_context(nc.psum_tensor("psH", [128, 512], F32))
    psK = b.ctx.enter_context(nc.psum_tensor("psK", [128, 1024], F32))

    def q0(q):
        for ci in range(NCI):
            b.inc("LD", q.dma_start(out=xs[ci][:], in_=xh[128 * ci:128 * ci + 128, :]), 16)
        for ci in range(NCI):
            b.inc("LD", q.dma_start(out=w9s[:, 9 * ci:9 * ci + 9],
                                    in_=w9[128 * ci:128 * ci + 128, :]), 16)
        b.inc("LD", q.dma_start(out=idf[:], in_=ident), 16)
    b.Q(q0)

    ldv = b.cnt["LD"]

    def v0(v):
        b.w(v, "LD", ldv)
        v.memset(onescol[:, 0:1], 1.0)
        v.memset(epscol[:, 0:1], LNEPS)
        v.memset(nhalfcol[:, 0:1], -0.5)
        b.inc("gVEc", v.memset(onerow[0:1, :], 1.0))
    b.V(v0)

    def s0(s):
        b.w(s, "LD", ldv)
        b.w(s, "gVEc", 1)
    b.S(s0)

    def t0(t):
        b.w(t, "LD", ldv)
        b.w(t, "gVEc", 1)
    b.T(t0)

    emit_ln_gate(b, xs, TT, idf, w9s, onescol, onerow, epscol, nhalfcol, psG, psH, psK,
                 o_xn, prob, mrow, murow, rrow, mu_b, r_b, "g")

    def q9(q):
        b.w(q, "gVEc")
        b.inc("ST", q.dma_start(out=o_pr, in_=prob[:, 0:NTJ * 8]), 16)
    b.Q(q9)

    def g9(g):
        b.w(g, "ST")
        b.dwait_all(g, "gXNS", NCI)
    b.G(g9)
    return b



def build_tau(cast=False):
    b = B(32 * 1024)
    nc = b.nc
    pp = nc.dram_tensor("pp", [16384, 8], F32, kind="ExternalInput").ap()
    grid0 = nc.dram_tensor("grid0", [1, TAU_W], F32, kind="ExternalInput").ap()
    o_tau = nc.dram_tensor("o_tau", [1, 1], F32, kind="ExternalOutput").ap()
    if cast:
        wc_in = nc.dram_tensor("wc_in", [2560, 2048], F32, kind="ExternalInput").ap()
        o_wc = nc.dram_tensor("o_wc", [2560, 2048], BF16, kind="ExternalOutput").ap()

    b.start()
    pps = b.alloc(128 * 8, F32)
    th = b.alloc(TAU_W, F32)
    thb = b.alloc(TAU_W, F32)
    msk = b.alloc(TAU_W * 128, BF16)
    red = b.alloc(TAU_W, F32)
    cnt = b.alloc(TAU_W, F32)
    m8 = b.alloc(TAU_W, F32)
    scr8 = b.alloc(TAU_W, F32)
    lo = b.alloc(4, F32)
    hi = b.alloc(4, F32)
    span = b.alloc(4, F32)
    iot = b.alloc(TAU_W, F32)
    onerow = b.alloc(128, F32)
    onescol = b.alloc(1, F32)

    psA = b.ctx.enter_context(nc.psum_tensor("psA", [128, 512], F32))

    def q0(q):
        b.inc("LD", q.dma_start(out=pps[:].rearrange("p (a z) -> p a z", z=8),
                                in_=pp.rearrange("(p a) z -> p a z", a=128)), 16)
        b.inc("LD", q.dma_start(out=iot[0:1, 0:TAU_W], in_=grid0), 16)
    b.Q(q0)
    ldv = b.cnt["LD"]

    if cast:
        def gw(g):
            b.inc("WC", g.dma_start(out=o_wc, in_=wc_in), 16)
        b.G(gw)

    def v0(v):
        b.w(v, "LD", ldv)
        v.memset(onescol[:, 0:1], 1.0)
        v.memset(onerow[0:1, :], 1.0)
        v.memset(lo[0:1, 0:1], 0.0)
        v.memset(hi[0:1, 0:1], 1.0)
        b.inc("VE", v.tensor_scalar(out=th[0:1, 0:TAU_W], in0=iot[0:1, 0:TAU_W],
                                    scalar1=1.0 / (TAU_W + 1), scalar2=None, op0=OP.mult))
    b.V(v0)

    ppv = pps[:].rearrange("p (a z) -> p a z", z=8)

    for it in range(TAU_ITERS):
        def t1(t, it=it):
            b.w(t, "VE")
            b.inc("PE", t.matmul(psA[:, 0:TAU_W], onerow[0:1, :], th[0:1, 0:TAU_W],
                                 start=True, stop=True))
        b.T(t1)

        def v1(v, it=it):
            b.w(v, "PE")
            b.inc("VE", v.tensor_copy(out=thb[:, 0:TAU_W], in_=psA[:, 0:TAU_W]))
            b.w(v, "VE")
            b.inc("VE", v.tensor_tensor(
                out=msk[:, 0:TAU_W * 128].rearrange("p (w a) -> p w a", a=128),
                in0=ppv[:, :, 0:1].squeeze(2).unsqueeze(1).to_broadcast([128, TAU_W, 128]),
                in1=thb[:, 0:TAU_W].unsqueeze(2).to_broadcast([128, TAU_W, 128]),
                op=OP.is_ge))
            b.w(v, "VE")
            b.inc("VE", v.reduce_sum(out=red[:, 0:TAU_W],
                                     in_=msk[:, 0:TAU_W * 128].rearrange("p (w a) -> p w a", a=128),
                                     axis=AX.X))
        b.V(v1)

        def t2(t, it=it):
            b.w(t, "VE")
            b.inc("PE", t.matmul(psA[0:1, 64:64 + TAU_W], onescol[:, 0:1], red[:, 0:TAU_W],
                                 start=True, stop=True))
        b.T(t2)

        def v2(v, it=it):
            b.w(v, "PE")
            b.inc("VE", v.tensor_copy(out=cnt[0:1, 0:TAU_W], in_=psA[0:1, 64:64 + TAU_W]))
            b.w(v, "VE")
            b.inc("VE", v.tensor_scalar(out=m8[0:1, 0:TAU_W], in0=cnt[0:1, 0:TAU_W],
                                        scalar1=2048.0, scalar2=None, op0=OP.is_ge))
            b.w(v, "VE")
            b.inc("VE", v.tensor_mul(out=scr8[0:1, 0:TAU_W], in0=th[0:1, 0:TAU_W],
                                     in1=m8[0:1, 0:TAU_W]))
            b.w(v, "VE")
            b.inc("VE", v.reduce_max(out=scr8[0:1, 0:1], in_=scr8[0:1, 0:TAU_W], axis=AX.X))
            b.w(v, "VE")
            b.inc("VE", v.tensor_max(out=lo[0:1, 0:1], in0=lo[0:1, 0:1], in1=scr8[0:1, 0:1]))
            b.w(v, "VE")
            b.inc("VE", v.scalar_tensor_tensor(out=scr8[0:1, 0:TAU_W], in0=m8[0:1, 0:TAU_W],
                                               scalar=BIG, in1=th[0:1, 0:TAU_W],
                                               op0=OP.mult, op1=OP.add))
            b.w(v, "VE")
            b.inc("VE", v.tensor_reduce(out=scr8[0:1, 1:2], in_=scr8[0:1, 0:TAU_W],
                                        axis=AX.X, op=OP.min))
            b.w(v, "VE")
            b.inc("VE", v.tensor_tensor(out=hi[0:1, 0:1], in0=hi[0:1, 0:1],
                                        in1=scr8[0:1, 1:2], op=OP.min))
            b.w(v, "VE")
            b.inc("VE", v.tensor_scalar(out=span[0:1, 0:1], in0=hi[0:1, 0:1],
                                        scalar1=lo[0:1, 0:1], scalar2=1.0 / (TAU_W + 1),
                                        op0=OP.subtract, op1=OP.mult))
            b.w(v, "VE")
            b.inc("VE", v.tensor_scalar(out=th[0:1, 0:TAU_W], in0=iot[0:1, 0:TAU_W],
                                        scalar1=span[0:1, 0:1], scalar2=lo[0:1, 0:1],
                                        op0=OP.mult, op1=OP.add))
        b.V(v2)

    def q9(q):
        b.w(q, "VE")
        b.inc("ST", q.dma_start(out=o_tau, in_=lo[0:1, 0:1]), 16)
    b.Q(q9)

    def g9(g):
        b.w(g, "ST", 16)
        if cast:
            b.w(g, "WC")
    b.G(g9)
    return b



def emit_routing(b, ntj, prob, tau_row, iota8_row, iotab_row, onescol, onerow,
                 ut128, psR, psT, psB, gate, idx, pfx, dbg=None):
    RV, RP = pfx + "RV", pfx + "RP"
    A = b.alloc(ntj * 8, F32)
    Cc = b.alloc(ntj * 8, F32)
    asn = b.alloc(ntj, F32)
    mB = b.alloc(ntj, F32)
    t2e = b.alloc(ntj, F32)
    rank = b.alloc(ntj, F32)
    taub = b.alloc(8, F32)
    iotab = b.alloc(8, F32)
    iotBb = b.alloc(8, F32)
    off = b.alloc(8 * ntj + 8, F32)
    slotf = b.alloc(ntj, F32)

    pv = prob[:, 0:ntj * 8].rearrange("p (a z) -> p a z", z=8)
    Av = A[:, 0:ntj * 8].rearrange("p (a z) -> p a z", z=8)
    Cv = Cc[:, 0:ntj * 8].rearrange("p (a z) -> p a z", z=8)

    def t0(t):
        b.inc(RP, t.matmul(psB[:, 0:24], onerow[0:1, :], tau_row[0:1, 0:24],
                           start=True, stop=True))
    b.T(t0)

    def v1(v):
        b.w(v, RP)
        v.tensor_copy(out=taub[:, 0:8], in_=psB[:, 0:8])
        v.tensor_copy(out=iotab[:, 0:8], in_=psB[:, 8:16])
        b.inc(RV, v.tensor_copy(out=iotBb[:, 0:8], in_=psB[:, 16:24]))
        b.w(v, RV)
        b.inc(RV, v.tensor_tensor(out=Av, in0=pv,
                                  in1=taub[:, 0:8].unsqueeze(1).to_broadcast([128, ntj, 8]),
                                  op=OP.is_ge))
        b.w(v, RV)
        b.inc(RV, v.reduce_max(out=asn[:, 0:ntj], in_=Av, axis=AX.X))
        b.w(v, RV)
        b.inc(RV, v.tensor_tensor(out=Av, in0=pv, in1=Av, op=OP.mult))
        b.w(v, RV)
        b.inc(RV, v.tensor_scalar(out=mB[:, 0:ntj], in0=asn[:, 0:ntj], scalar1=-1.0,
                                  scalar2=1.0, op0=OP.mult, op1=OP.add))
        b.w(v, RV)
        b.inc(RV, v.tensor_tensor(out=Cv, in0=pv,
                                  in1=mB[:, 0:ntj].unsqueeze(2).to_broadcast([128, ntj, 8]),
                                  op=OP.mult))
        b.w(v, RV)
        b.inc(RV, v.tensor_tensor(out=Cv, in0=Cv, in1=Av, op=OP.add))
        b.w(v, RV)
        b.inc(RV, v.reduce_max(out=mB[:, 0:ntj], in_=Cv, axis=AX.X))
        b.w(v, RV)
        b.inc(RV, v.tensor_tensor(out=Cv, in0=Cv,
                                  in1=mB[:, 0:ntj].unsqueeze(2).to_broadcast([128, ntj, 8]),
                                  op=OP.is_equal))
        b.w(v, RV)
        b.inc(RV, v.scalar_tensor_tensor(out=Cv, in0=Cv, scalar=-1024.0,
                                         in1=iotBb[:, 0:8].unsqueeze(1).to_broadcast([128, ntj, 8]),
                                         op0=OP.mult, op1=OP.add))
        b.w(v, RV)
        b.inc(RV, v.tensor_reduce(out=t2e[:, 0:ntj], in_=Cv, axis=AX.X, op=OP.min))
        b.w(v, RV)
        b.inc(RV, v.tensor_tensor(out=Cv,
                                  in0=iotab[:, 0:8].unsqueeze(1).to_broadcast([128, ntj, 8]),
                                  in1=t2e[:, 0:ntj].unsqueeze(2).to_broadcast([128, ntj, 8]),
                                  op=OP.is_equal))
        b.w(v, RV)
        b.inc(RV, v.tensor_tensor(out=Av, in0=pv, in1=Cv, op=OP.mult))
        b.w(v, RV)
        b.inc(RV, v.reduce_sum(out=mB[:, 0:ntj], in_=Av, axis=AX.X))
        b.w(v, RV)
        b.inc(RV, v.tensor_scalar(out=gate[:, 0:ntj], in0=mB[:, 0:ntj], scalar1=0.5,
                                  scalar2=None, op0=OP.mult))
    b.V(v1)

    rv_onehot = b.cnt[RV]
    Crank = b.alloc(ntj * 8, F32)
    tsrow = b.alloc(ntj * 8, F32)
    offb = b.alloc(ntj * 8, F32)
    rp0 = b.cnt[RP]

    def t1(t):
        b.w(t, RV, rv_onehot)
        for tj in range(ntj):
            if tj >= 2:
                b.w(t, RV, rv_onehot + tj - 1)
            b.inc(RP, t.matmul([psR, psT][tj % 2][:, 0:8], ut128[:], Cv[:, tj, :],
                               start=True, stop=True))
    b.T(t1)

    def v1b(v):
        for tj in range(ntj):
            b.w(v, RP, rp0 + tj + 1)
            b.inc(RV, v.tensor_copy(out=Crank[:, 8 * tj:8 * tj + 8],
                                    in_=[psR, psT][tj % 2][:, 0:8]))
    b.V(v1b)

    rv_crank = b.cnt[RV]
    rp_crank = b.cnt[RP]

    def t1c(t):
        for tj in range(ntj):
            if tj >= 2:
                b.w(t, RV, rv_crank + tj - 1)
            else:
                b.w(t, RV, rv_crank)
            b.inc(RP, t.matmul([psR, psT][tj % 2][0:1, 0:8], onescol[:, 0:1],
                               Cv[:, tj, :], start=True, stop=True))
    b.T(t1c)

    def v2(v):
        for tj in range(ntj):
            b.w(v, RP, rp_crank + tj + 1)
            b.inc(RV, v.tensor_copy(out=tsrow[0:1, 8 * tj:8 * tj + 8],
                                    in_=[psR, psT][tj % 2][0:1, 0:8]))
        b.inc(RV, v.memset(off[0:1, 0:8], 0.0))
        for j in range(1, ntj):
            b.w(v, RV)
            b.inc(RV, v.tensor_tensor(out=off[0:1, 8 * j:8 * j + 8],
                                      in0=off[0:1, 8 * (j - 1):8 * j],
                                      in1=tsrow[0:1, 8 * (j - 1):8 * j], op=OP.add))
    b.V(v2)

    def t2(t):
        b.w(t, pfx + "RV")
        b.inc(RP, t.matmul(psB[:, 0:8 * ntj], onerow[0:1, :], off[0:1, 0:8 * ntj],
                           start=True, stop=True))
    b.T(t2)

    def v2b(v):
        b.w(v, RP)
        b.inc(RV, v.tensor_copy(out=offb[:, 0:8 * ntj], in_=psB[:, 0:8 * ntj]))
    b.V(v2b)

    if dbg is not None:
        def q_dbg(q):
            b.w(q, RV)
            b.inc("DBGR", q.dma_start(out=dbg["taub"], in_=taub[:, 0:8]), 16)
            b.inc("DBGR", q.dma_start(out=dbg["iotab"], in_=iotab[:, 0:8]), 16)
            b.inc("DBGR", q.dma_start(out=dbg["iotBb"], in_=iotBb[:, 0:8]), 16)
            b.inc("DBGR", q.dma_start(out=dbg["t2e"], in_=t2e[:, 0:ntj]), 16)
            b.inc("DBGR", q.dma_start(out=dbg["asn"], in_=asn[:, 0:ntj]), 16)
            b.inc("DBGR", q.dma_start(out=dbg["crank"], in_=Crank[:, 0:8 * ntj]), 16)
            b.inc("DBGR", q.dma_start(out=dbg["offb"], in_=offb[:, 0:8 * ntj]), 16)
        b.Q(q_dbg)

    def v3(v):
        b.w(v, RV)
        if dbg is not None:
            b.w(v, "DBGR")
        b.inc(RV, v.tensor_tensor(out=offb[:, 0:8 * ntj], in0=offb[:, 0:8 * ntj],
                                  in1=Crank[:, 0:8 * ntj], op=OP.add))
        b.w(v, RV)
        b.inc(RV, v.tensor_tensor(out=Av, in0=Cv,
                                  in1=offb[:, 0:8 * ntj].rearrange("p (a z) -> p a z", z=8),
                                  op=OP.mult))
        b.w(v, RV)
        b.inc(RV, v.reduce_sum(out=rank[:, 0:ntj], in_=Av, axis=AX.X))
        b.w(v, RV)
        b.inc(RV, v.scalar_tensor_tensor(out=slotf[:, 0:ntj], in0=t2e[:, 0:ntj],
                                         scalar=float(CAP), in1=rank[:, 0:ntj],
                                         op0=OP.mult, op1=OP.add))
        b.w(v, RV)
        b.inc(RV, v.tensor_scalar(out=mB[:, 0:ntj], in0=rank[:, 0:ntj],
                                  scalar1=float(CAP), scalar2=None, op0=OP.is_ge))
        b.w(v, RV)
        b.inc(RV, v.scalar_tensor_tensor(out=slotf[:, 0:ntj], in0=mB[:, 0:ntj],
                                         scalar=65536.0, in1=slotf[:, 0:ntj],
                                         op0=OP.mult, op1=OP.add))
        b.w(v, RV)
        b.inc(RV, v.tensor_copy(out=idx[:, 0:ntj], in_=slotf[:, 0:ntj]))
    b.V(v3)



def emit_ffn(b, ntj, xn_hbm, idx, w1b, w2b, frame, retf,
             ident_f32, psW0, psW1, psO0, psO1, pfx, fstage=3):
    nc = b.nc
    TPS, SCT, XES, WL1, WL2, FPE, GEV, OEV, RST = (
        pfx + s for s in ["TPS", "SCT", "XES", "WL1", "WL2", "FPE",
                          "GEV", "OEV", "RST"])
    xtm = [b.alloc(1024, BF16) for _ in range(2)]
    x_e = [b.alloc(SLOTS, BF16) for _ in range(NCI)]
    w1sb = b.alloc(NCI * 2048, BF16)
    w2sb = [b.alloc(NFI * 1024, BF16) for _ in range(2)]
    gh = b.alloc(NFI * CAP, BF16)
    otm = [b.alloc(1024, BF16) for _ in range(2)]

    def v_z(v):
        b.inc(pfx + "FRV", v.memset(gh[:, 0:1024], 0.0))
    b.V(v_z)

    def q_z(q):
        b.w(q, pfx + "FRV", b.cnt[pfx + "FRV"])
        for j in range(SLOTS // 128):
            b.inc(pfx + "FRZ", q.dma_start(out=frame[128 * j:128 * (j + 1), :],
                                           in_=gh[:, 0:1024].bitcast(BF16)
                                           if False else gh[:, 0:1024]), 16)
    b.G(q_z)

    def q0(q):
        for tj in range(ntj):
            if tj >= 2:
                b.dwait(q, SCT, tj - 2)
            b.dinc(TPS, tj, q.dma_start(out=xtm[tj % 2][:],
                                        in_=xn_hbm[:, 128 * tj:128 * tj + 128],
                                        transpose=True))
    b.Q(q0)

    def g0(g):
        b.w(g, "rRV")
        b.w(g, pfx + "FRZ", 16 * (SLOTS // 128))
        for tj in range(ntj):
            b.dwait(g, TPS, tj)
            if tj >= 1:
                b.dwait(g, SCT, tj - 1)
            b.dinc(SCT, tj, g.indirect_dma_start(
                out=frame, out_offset=bass.IndirectOffsetOnAxis(idx[:, tj:tj + 1], 0),
                in_=xtm[tj % 2][:], in_offset=None,
                bounds_check=SLOTS - 1, oob_is_err=False))
    b.G(g0)

    if fstage <= 1:
        return

    def q1(s):
        b.dwait_all(s, SCT, ntj)
        for ci in range(NCI):
            b.inc(XES, s.dma_start(out=x_e[ci][:],
                                   in_=frame[:, 128 * ci:128 * ci + 128],
                                   transpose=True), 16)
    b.S(q1)

    if fstage <= 2:
        return

    def q23(q):
        for ci in range(NCI):
            b.dinc(WL1, 0, q.dma_start(out=w1sb[:, 2048 * ci:2048 * (ci + 1)],
                                       in_=w1b[128 * ci:128 * (ci + 1), :]))
        for e in range(E):
            if e + 1 < E:
                b.w(q, FPE, e * 19 + 16)
                for ci in range(NCI):
                    b.dinc(WL1, e + 1, q.dma_start(
                        out=w1sb[:, 2048 * ci:2048 * (ci + 1)],
                        in_=w1b[1024 * (e + 1) + 128 * ci:1024 * (e + 1) + 128 * (ci + 1), :]))
            for tc in range(3):
                b.w(q, OEV, e * 3 + tc + 1)
                b.dinc(RST, e * 3 + tc, q.dma_start(
                    out=retf[CAP * e + 128 * tc:CAP * e + 128 * (tc + 1), :],
                    in_=otm[(e * 3 + tc) % 2][:]))
    b.Q(q23)

    def g23(g):
        for e in range(E):
            if e >= 2:
                b.w(g, FPE, (e - 1) * 19)
            for fi in range(NFI):
                b.dinc(WL2, e, g.dma_start(
                    out=w2sb[e % 2][:, 1024 * fi:1024 * (fi + 1)],
                    in_=w2b[2048 * e + 128 * fi:2048 * e + 128 * (fi + 1), :]))
    b.G(g23)

    def t1(t):
        b.w(t, XES)
        for e in range(E):
            b.dwait(t, WL1, e, per=128)
            psW = [psW0, psW1]
            for fi in range(NFI):
                need = e * NFI + fi - 1
                if need >= 1:
                    b.w(t, GEV, need)
                for ci in range(NCI):
                    ii = t.matmul(psW[fi % 2][:, 0:CAP],
                                  w1sb[:, 2048 * ci + 128 * fi:2048 * ci + 128 * fi + 128],
                                  x_e[ci][:, CAP * e:CAP * (e + 1)],
                                  start=(ci == 0), stop=(ci == NCI - 1),
                                  skip_group_check=True)
                b.inc(FPE, ii)
            b.dwait(t, WL2, e, per=256)
            b.w(t, GEV, (e + 1) * NFI)
            psO = [psO0, psO1]
            for tc in range(3):
                need = e * 3 + tc - 1
                if need >= 1:
                    b.w(t, OEV, need)
                for fi in range(NFI):
                    lhs = gh[:, CAP * fi + 128 * tc:CAP * fi + 128 * tc + 128]
                    t.matmul(psO[tc % 2][:, 0:512], lhs,
                             w2sb[e % 2][:, 1024 * fi:1024 * fi + 512],
                             start=(fi == 0), stop=(fi == NFI - 1), skip_group_check=True)
                    ii = t.matmul(psO[tc % 2][:, 512:1024], lhs,
                                  w2sb[e % 2][:, 1024 * fi + 512:1024 * fi + 1024],
                                  start=(fi == 0), stop=(fi == NFI - 1), skip_group_check=True)
                b.inc(FPE, ii)
    b.T(t1)

    def s1(s):
        for e in range(E):
            for fi in range(NFI):
                b.w(s, FPE, e * 19 + fi + 1)
                b.inc(GEV, s.activation(out=gh[:, CAP * fi:CAP * (fi + 1)],
                                        in_=psW0[:, 0:CAP] if fi % 2 == 0 else psW1[:, 0:CAP],
                                        func=AF.Gelu))
    b.S(s1)

    def v1(v):
        for e in range(E):
            for tc in range(3):
                k = e * 3 + tc
                if k >= 2:
                    b.dwait(v, RST, k - 2)
                b.w(v, FPE, e * 19 + NFI + tc + 1)
                b.inc(OEV, v.tensor_copy(out=otm[k % 2][:],
                                          in_=psO0[:, 0:1024] if tc % 2 == 0 else psO1[:, 0:1024]))
    b.V(v1)




TOKCH = 768
NCH = TT // TOKCH


def build_main(stage=4, FSTAGE=3, DBG=False):
    b = B(176 * 1024)
    nc = b.nc
    xh = nc.dram_tensor("xh", [C, TT], F32, kind="ExternalInput").ap()
    xn1h = nc.dram_tensor("xn1h", [C, TT], BF16, kind="ExternalInput").ap()
    pr1 = nc.dram_tensor("pr1", [128, NTJ * 8], F32, kind="ExternalInput").ap()
    tau1 = nc.dram_tensor("tau1", [1, 8], F32, kind="ExternalInput").ap()
    w9 = nc.dram_tensor("w9", [C, 9], F32, kind="ExternalInput").ap()
    ident = nc.dram_tensor("ident", [128, 128], F32, kind="ExternalInput").ap()
    ut = nc.dram_tensor("ut", [128, 128], F32, kind="ExternalInput").ap()
    iota8 = nc.dram_tensor("iota8", [1, 8], F32, kind="ExternalInput").ap()
    iotaB = nc.dram_tensor("iotaB", [1, 8], F32, kind="ExternalInput").ap()
    hmask = nc.dram_tensor("hmask", [1, HALO], F32, kind="ExternalInput").ap()
    mixk = nc.dram_tensor("mixk", [128, 8], F32, kind="ExternalInput").ap()
    mixv = nc.dram_tensor("mixv", [128, 8], F32, kind="ExternalInput").ap()
    mixr = nc.dram_tensor("mixr", [128, 8], F32, kind="ExternalInput").ap()
    tdk = nc.dram_tensor("tdk", [128, 8], F32, kind="ExternalInput").ap()
    tfk = nc.dram_tensor("tfk", [128, 8], F32, kind="ExternalInput").ap()
    lnwk = nc.dram_tensor("lnwk", [128, 8], F32, kind="ExternalInput").ap()
    lnbk = nc.dram_tensor("lnbk", [128, 8], F32, kind="ExternalInput").ap()
    w1b = nc.dram_tensor("w1b", [8 * C, FF], BF16, kind="ExternalInput").ap()
    w2b = nc.dram_tensor("w2b", [8 * FF, C], BF16, kind="ExternalInput").ap()
    wkb = nc.dram_tensor("wkb", [C, C], BF16, kind="ExternalInput").ap()
    wvb = nc.dram_tensor("wvb", [C, C], BF16, kind="ExternalInput").ap()
    wrb = nc.dram_tensor("wrb", [C, C], BF16, kind="ExternalInput").ap()
    wob = nc.dram_tensor("wob", [C, C], BF16, kind="ExternalInput").ap()

    o_x2 = nc.dram_tensor("o_x2", [C, TOWN], F32, kind="ExternalOutput").ap()
    o_xn2 = nc.dram_tensor("o_xn2", [C, TOWN], BF16, kind="ExternalOutput").ap()
    o_pr2 = nc.dram_tensor("o_pr2", [128, (TOWN // 128) * 8], F32, kind="ExternalOutput").ap()

    frame = nc.dram_tensor("frame", [SLOTS, C], BF16, kind="Internal").ap()
    retf = nc.dram_tensor("retf", [SLOTS, C], BF16, kind="Internal").ap()
    tmscr = nc.dram_tensor("tmscr", [TT, C], BF16, kind="Internal").ap()
    x1scr = nc.dram_tensor("x1scr", [C, TT], F32, kind="Internal").ap()
    wspill = nc.dram_tensor("wspill", [C, TT], F32, kind="Internal").ap()
    mrow = nc.dram_tensor("mrow", [2, TT], F32, kind="Internal").ap()
    if DBG:
        rdbg = {k: nc.dram_tensor("o_r" + k, [128, w], F32, kind="ExternalOutput").ap()
                for k, w in [("taub", 8), ("iotab", 8), ("iotBb", 8), ("t2e", NTJ),
                             ("asn", NTJ), ("crank", NTJ * 8), ("offb", NTJ * 8)]}
        o_dx1 = nc.dram_tensor("o_dx1", [C, TT], F32, kind="ExternalOutput").ap()
        o_dw = nc.dram_tensor("o_dw", [C, TT], F32, kind="ExternalOutput").ap()
        o_dk = nc.dram_tensor("o_dk", [C, TOKCH], F32, kind="ExternalOutput").ap()
        o_dv = nc.dram_tensor("o_dv", [C, TOKCH], F32, kind="ExternalOutput").ap()
        o_dr = nc.dram_tensor("o_dr", [C, TOKCH], F32, kind="ExternalOutput").ap()
        o_dxt = nc.dram_tensor("o_dxt", [C, TOKCH], F32, kind="ExternalOutput").ap()
        o_didx = nc.dram_tensor("o_didx", [128, NTJ], I32, kind="ExternalOutput").ap()
        o_dgate = nc.dram_tensor("o_dgate", [128, NTJ], F32, kind="ExternalOutput").ap()

    b.start()
    PS_A = b.ctx.enter_context(nc.psum_tensor("PS_A", [128, 512], F32))
    PS_B = b.ctx.enter_context(nc.psum_tensor("PS_B", [128, 512], F32))
    PS_C = b.ctx.enter_context(nc.psum_tensor("PS_C", [128, 512], F32))
    PS_D = b.ctx.enter_context(nc.psum_tensor("PS_D", [128, 512], F32))
    PS_E = b.ctx.enter_context(nc.psum_tensor("PS_E", [128, 1024], F32))
    PS_F = b.ctx.enter_context(nc.psum_tensor("PS_F", [128, 1024], F32))

    prob = b.alloc(NTJ * 8, F32)
    gate = b.alloc(NTJ, F32)
    idx = b.alloc(NTJ, I32)
    trow = b.alloc(24, F32)
    taur = trow[:, 0:8]
    io8r = trow[:, 8:16]
    ioBr = trow[:, 16:24]
    onescol = b.alloc(1, F32)
    onerow = b.alloc(128, F32)
    onescol_bf = b.alloc(1, BF16)
    epscol = b.alloc(1, F32)
    nhalfcol = b.alloc(1, F32)
    idf = b.alloc(128, F32)
    utsb = b.alloc(128, F32)
    mixs = {"k": b.alloc(8, F32), "v": b.alloc(8, F32), "r": b.alloc(8, F32)}
    mix1m = {"k": b.alloc(8, F32), "v": b.alloc(8, F32), "r": b.alloc(8, F32)}
    dk = b.alloc(8, F32)
    tf = b.alloc(8, F32)
    lnw = b.alloc(8, F32)
    lnb = b.alloc(8, F32)
    maskb = b.alloc(HALO, F32)
    carry = b.alloc(8, F32)
    bndA = b.alloc(8, BF16)
    bndB = b.alloc(8, BF16)

    def q_init(q):
        b.inc("LD", q.dma_start(out=prob[:, :], in_=pr1), 16)
        b.inc("LD", q.dma_start(out=taur[0:1, 0:8], in_=tau1), 16)
        b.inc("LD", q.dma_start(out=io8r[0:1, 0:8], in_=iota8), 16)
        b.inc("LD", q.dma_start(out=ioBr[0:1, 0:8], in_=iotaB), 16)
        b.inc("LD", q.dma_start(out=idf[:], in_=ident), 16)
        b.inc("LD", q.dma_start(out=utsb[:], in_=ut), 16)
        b.inc("LD", q.dma_start(out=mixs["k"][:, 0:8], in_=mixk), 16)
        b.inc("LD", q.dma_start(out=mixs["v"][:, 0:8], in_=mixv), 16)
        b.inc("LD", q.dma_start(out=mixs["r"][:, 0:8], in_=mixr), 16)
        b.inc("LD", q.dma_start(out=dk[:, 0:8], in_=tdk), 16)
        b.inc("LD", q.dma_start(out=tf[:, 0:8], in_=tfk), 16)
        b.inc("LD", q.dma_start(out=lnw[:, 0:8], in_=lnwk), 16)
        b.inc("LD", q.dma_start(out=lnb[:, 0:8], in_=lnbk), 16)
    b.Q(q_init)
    ld0 = b.cnt["LD"]

    def v_init(v):
        b.w(v, "LD", ld0)
        v.memset(onescol[:, 0:1], 1.0)
        v.memset(onescol_bf[:, 0:1], 1.0)
        v.memset(epscol[:, 0:1], LNEPS)
        v.memset(nhalfcol[:, 0:1], -0.5)
        v.memset(onerow[0:1, :], 1.0)
        v.memset(carry[:, 0:8], 0.0)
        v.memset(bndA[:, 0:8], 0.0)
        for p in ("k", "v", "r"):
            v.tensor_scalar(out=mix1m[p][:, 0:8], in0=mixs[p][:, 0:8],
                            scalar1=-1.0, scalar2=1.0, op0=OP.mult, op1=OP.add)
        b.inc("INI", v.engine_nop())
    b.V(v_init)

    def s_init(s):
        b.w(s, "INI", 1)
        s.activation(out=dk[:, 0:8], in_=dk[:, 0:8], func=AF.Sigmoid)
        b.inc("INI2", s.activation(out=tf[:, 0:8], in_=tf[:, 0:8], func=AF.Exp))
    b.S(s_init)

    def q_mask(q):
        b.inc("LDM", q.dma_start(out=maskb[0:1, 0:HALO], in_=hmask), 16)
    b.Q(q_mask)
    ld1 = b.cnt["LD"]

    maskbc = b.alloc(HALO, F32)

    def t_maskb(t):
        b.w(t, "LD", ld0)
        b.w(t, "LDM", 16)
        b.w(t, "INI", 1)
        b.inc("MKP", t.matmul(PS_F[:, 0:HALO], onerow[0:1, :], maskb[0:1, 0:HALO],
                              start=True, stop=True))
    b.T(t_maskb)

    def v_maskb(v):
        b.w(v, "MKP", 1)
        b.inc("MKV", v.tensor_copy(out=maskbc[:, 0:HALO], in_=PS_F[:, 0:HALO]))
    b.V(v_maskb)

    def pre_r(eng):
        b.w(eng, "LD", ld0)
        b.w(eng, "INI", 1)
        b.w(eng, "MKV", 1)
    for sec in (b.V, b.S, b.T, b.G, b.Q):
        sec(pre_r)

    if stage <= -1:
        def g_finm1(g):
            b.w(g, "MKV", 1)
        b.G(g_finm1)
        return b
    emit_routing(b, NTJ, prob, trow, io8r, ioBr, onescol, onerow, utsb,
                 PS_C, PS_D, PS_A, gate, idx, "r", dbg=(rdbg if DBG else None))
    if DBG:
        def q_dbgr(q):
            b.w(q, "rRV")
            b.inc("DBG1", q.dma_start(out=o_didx, in_=idx[:, 0:NTJ]), 16)
            b.inc("DBG1", q.dma_start(out=o_dgate, in_=gate[:, 0:NTJ]), 16)
        b.Q(q_dbgr)
    if stage <= 0:
        def g_fin0(g):
            b.w(g, "rRV")
        b.G(g_fin0)
        return b
    b.push()
    emit_ffn(b, NTJ, xn1h, idx, w1b, w2b, frame, retf,
             idf, PS_A, PS_B, PS_E, PS_F, "f", fstage=FSTAGE)
    b.pop()

    if stage <= 1:
        def g_fin1(g):
            if FSTAGE >= 3:
                b.dwait_all(g, "fRST", E * 3)
            elif FSTAGE == 2:
                b.w(g, "fXES")
            else:
                b.dwait_all(g, "fSCT", NTJ)
        b.G(g_fin1)
        return b

    b.push()
    xht = [b.alloc(TT, F32) for _ in range(2)]
    ffnt = [b.alloc(TT, BF16) for _ in range(2)]
    x1t = [b.alloc(TT, F32) for _ in range(2)]
    gtm = [b.ctx.enter_context(b.nc.sbuf_tensor("gtm%d" % i, [128, 1024], BF16)).ap()
           for i in range(2)]

    gts = [b.alloc(1024, BF16) for _ in range(2)]

    def g_gb(g):
        b.w(g, "fRSTA")
        b.w(g, "fRSTB")
        for tj in range(NTJ):
            if tj >= 2:
                b.w(g, "GBV", tj - 1)
            b.dinc("GBK", tj, g.indirect_dma_start(
                out=gtm[tj % 2][:], out_offset=None,
                in_=retf, in_offset=bass.IndirectOffsetOnAxis(idx[:, tj:tj + 1], 0),
                bounds_check=SLOTS - 1, oob_is_err=False))
    b.G(g_gb)

    def v_gb(v):
        for tj in range(NTJ):
            b.dwait(v, "GBK", tj)
            if tj >= 2:
                b.dwait(v, "GBS", tj - 2)
            b.inc("GBV", v.tensor_scalar(out=gts[tj % 2][:], in0=gtm[tj % 2][:],
                                         scalar1=gate[:, tj:tj + 1], scalar2=None,
                                         op0=OP.mult))
    b.V(v_gb)

    def q_gb2(q):
        for tj in range(NTJ):
            b.w(q, "GBV", tj + 1)
            b.dinc("GBS", tj, q.dma_start(out=tmscr[128 * tj:128 * (tj + 1), :],
                                          in_=gts[tj % 2][:]))
    b.Q(q_gb2)

    def q_x1(q):
        b.dwait_all(q, "GBS", NTJ)
        for ci in range(NCI + 1):
            if ci < NCI:
                if ci >= 2:
                    b.w(q, "X1V", ci - 1)
                q.dma_start(out=ffnt[ci % 2][:],
                            in_=tmscr[:, 128 * ci:128 * (ci + 1)],
                            transpose=True).then_inc(b.sem("X1T" + "AB"[ci % 2]), 16)
                b.dinc("X1T", ci, q.dma_start(out=xht[ci % 2][:],
                                              in_=xh[128 * ci:128 * (ci + 1), :]))
                b.cnt["X1T" + "AB"[ci % 2]] += 16
            if ci >= 1:
                b.w(q, "X1V", ci)
                b.dinc("X1S", ci - 1, q.dma_start(
                    out=x1scr[128 * (ci - 1):128 * ci, :], in_=x1t[(ci - 1) % 2][:]))
                if DBG:
                    b.inc("DBG1", q.dma_start(
                        out=o_dx1[128 * (ci - 1):128 * ci, :], in_=x1t[(ci - 1) % 2][:]), 16)
    b.Q(q_x1)

    def v_x1(v):
        for ci in range(NCI):
            b.dwait(v, "X1T", ci, per=32)
            if ci >= 2:
                b.dwait(v, "X1S", ci - 2)
            b.inc("X1V", v.tensor_tensor(out=x1t[ci % 2][:], in0=xht[ci % 2][:],
                                         in1=ffnt[ci % 2][:], op=OP.add))
    b.V(v_x1)

    if stage <= 2:
        def g_fin2(g):
            b.dwait_all(g, "X1S", NCI)
        b.G(g_fin2)
        return b

    b.pop()
    b.push()
    xt = [b.alloc(TOKCH, BF16) for _ in range(NCI)]
    vb = [b.alloc(TOKCH, BF16) for _ in range(NCI)]
    mx = [b.alloc(TOKCH, BF16) for _ in range(NCI)]
    K = [b.alloc(TOKCH, BF16) for _ in range(NCI)]
    Vt = [b.alloc(TOKCH, BF16) for _ in range(NCI)]
    R = [b.alloc(TOKCH, BF16) for _ in range(NCI)]
    Ssc = b.alloc(TOKCH, F32)
    wproj = [b.alloc(NCI * 1024, BF16) for _ in range(2)]
    st_in = [b.alloc(TOKCH, F32) for _ in range(2)]
    st_bf = [b.alloc(TOKCH, BF16) for _ in range(2)]
    st_sq = [b.alloc(TOKCH, BF16) for _ in range(2)]
    vf32 = b.alloc(TOKCH, F32)
    wtile = [b.alloc(TOKCH, F32) for _ in range(2)]
    srow = b.alloc(TOKCH, F32)
    qrow = b.alloc(TOKCH, F32)
    mu_bc = b.alloc(TOKCH, F32)
    r_bc = b.alloc(TOKCH, F32)
    x2t = [b.alloc(TOKCH, F32) for _ in range(2)]

    SCH = [(0, 512), (512, TOKCH - 512)]

    def stats_from(b, sfx, src_ready, n_ready_per_ci, bf_tiles, sq_tiles, src_base):
        for _n in (sfx + "P", sfx + "V", sfx + "S", src_ready):
            b.sem(_n)
        B0 = {n: b.cnt[n] for n in (sfx + "P", sfx + "V", sfx + "S", src_ready)}
        B0[src_ready] = src_base
        def tS(t):
            first = [True] * 4
            last = None
            for ci in range(NCI):
                b.w(t, src_ready, B0[src_ready] + n_ready_per_ci * (ci + 1))
                for kk, (off, cw) in enumerate(SCH):
                    sreg = (PS_A if kk == 0 else PS_B)[0:1, 0:cw]
                    qreg = PS_E[0:1, 0:cw] if kk == 0 else PS_E[0:1, 512:512 + cw]
                    t.matmul(sreg, onescol_bf[:, 0:1], bf_tiles(ci)[:, off:off + cw],
                             start=first[kk], stop=(ci == NCI - 1), skip_group_check=True)
                    first[kk] = False
                    last = t.matmul(qreg, onescol_bf[:, 0:1], sq_tiles(ci)[:, off:off + cw],
                                    start=first[kk + 2], stop=(ci == NCI - 1),
                                    skip_group_check=True)
                    first[kk + 2] = False
                b.inc(sfx + "P", last)
        b.T(tS)

        def vS(v):
            b.w(v, sfx + "P", B0[sfx + "P"] + NCI)
            for kk, (off, cw) in enumerate(SCH):
                sreg = (PS_A if kk == 0 else PS_B)[0:1, 0:cw]
                qreg = PS_E[0:1, 0:cw] if kk == 0 else PS_E[0:1, 512:512 + cw]
                v.tensor_scalar(out=srow[0:1, off:off + cw], in0=sreg,
                                scalar1=1.0 / C, scalar2=None, op0=OP.mult)
                b.inc(sfx + "V", v.tensor_scalar(out=qrow[0:1, off:off + cw], in0=qreg,
                                                 scalar1=1.0 / C, scalar2=None, op0=OP.mult))
            b.w(v, sfx + "V")
            b.inc(sfx + "V", v.tensor_tensor(out=vf32[0:1, 0:TOKCH], in0=srow[0:1, 0:TOKCH],
                                             in1=srow[0:1, 0:TOKCH], op=OP.mult))
            b.w(v, sfx + "V")
            b.inc(sfx + "V", v.tensor_sub(out=qrow[0:1, 0:TOKCH], in0=qrow[0:1, 0:TOKCH],
                                          in1=vf32[0:1, 0:TOKCH]))
        b.V(vS)

        def sS(s):
            b.w(s, sfx + "V")
            b.inc(sfx + "S", s.activation(out=qrow[0:1, 0:TOKCH], in_=qrow[0:1, 0:TOKCH],
                                          func=AF.Ln, bias=epscol[0:1, 0:1]))
            b.w(s, sfx + "S")
            b.inc(sfx + "S", s.activation(out=qrow[0:1, 0:TOKCH], in_=qrow[0:1, 0:TOKCH],
                                          func=AF.Exp, scale=nhalfcol[0:1, 0:1]))
        b.S(sS)

        def tB(t):
            b.w(t, sfx + "S", B0[sfx + "S"] + 2)
            t.matmul(PS_F[:, 0:512], onerow[0:1, :], srow[0:1, 0:512], start=True, stop=True)
            t.matmul(PS_F[:, 512:512 + SCH[1][1]], onerow[0:1, :],
                     srow[0:1, 512:TOKCH], start=True, stop=True)
            t.matmul(PS_A[:, 0:512], onerow[0:1, :], qrow[0:1, 0:512], start=True, stop=True)
            b.inc(sfx + "P", t.matmul(PS_B[:, 0:SCH[1][1]], onerow[0:1, :],
                                      qrow[0:1, 512:TOKCH], start=True, stop=True))
        b.T(tB)

        def vB(v):
            b.w(v, sfx + "P", B0[sfx + "P"] + NCI + 1)
            v.tensor_copy(out=mu_bc[:, 0:512], in_=PS_F[:, 0:512])
            v.tensor_copy(out=mu_bc[:, 512:TOKCH], in_=PS_F[:, 512:512 + SCH[1][1]])
            v.tensor_copy(out=r_bc[:, 0:512], in_=PS_A[:, 0:512])
            b.inc(sfx + "V", v.tensor_copy(out=r_bc[:, 512:TOKCH], in_=PS_B[:, 0:SCH[1][1]]))
        b.V(vB)

    CC_OWN = [(HALO, TOKCH, 0), (0, TOKCH, TOKCH - HALO), (0, TOKCH, 2 * TOKCH - HALO)]

    CHNAMES = ["l2c", "l2q", "l2P", "l2V", "l2S", "wvV",
               "l3q", "l3P", "l3V", "l3S", "xtV", "mxk", "mxv", "mxr",
               "pmk", "pmv", "pmr", "pek", "pev", "per",
               "wpk", "wpv", "wpr", "wpo", "sc", "ym", "ye"]
    for cc in range(NCH):
        c0 = cc * TOKCH
        z = ""
        z2 = ""
        for _n in CHNAMES:
            b.sem(_n)
        B0 = {n: b.cnt[n] for n in CHNAMES}
        def W(eng, n, val, B0=B0):
            b.w(eng, n, B0[n] + val)

        def q_l2(q, c0=c0, z=z):
            if cc == 0:
                b.dwait_all(q, "X1S", NCI)
            for ci in range(NCI):
                if ci >= 2:
                    W(q, "l2c", ci - 1)
                b.dinc("l2l" + z2, cc * NCI + ci, q.dma_start(
                    out=st_in[ci % 2][:, 0:TOKCH],
                    in_=x1scr[128 * ci:128 * (ci + 1), c0:c0 + TOKCH]))
        b.Q(q_l2)

        def v_l2(v, z=z):
            for ci in range(NCI):
                b.dwait(v, "l2l", cc * NCI + ci)
                if ci >= 2:
                    W(v, "l2P", ci - 1)
                b.inc("l2c", v.tensor_copy(out=st_bf[ci % 2][:, 0:TOKCH],
                                               in_=st_in[ci % 2][:, 0:TOKCH]))
        b.V(v_l2)

        def s_l2(s, z=z):
            for ci in range(NCI):
                W(s, "l2c", ci + 1)
                if ci >= 2:
                    W(s, "l2P", ci - 1)
                b.inc("l2q", s.activation(out=st_sq[ci % 2][:, 0:TOKCH],
                                              in_=st_bf[ci % 2][:, 0:TOKCH], func=AF.Square))
        b.S(s_l2)

        stats_from(b, "l2", "l2q", 1,
                   lambda ci: st_bf[ci % 2], lambda ci: st_sq[ci % 2],
                   b.cnt["l2q"] - NCI)

        def q_wv(q, c0=c0, z=z):
            for ci in range(NCI + 1):
                if ci < NCI:
                    if ci >= 2:
                        W(q, "wvV", 4 * (ci - 1))
                    else:
                        W(q, "l2c", NCI)
                    b.dinc("wvL" + z2, cc * NCI + ci, q.dma_start(
                        out=st_in[ci % 2][:, 0:TOKCH],
                        in_=x1scr[128 * ci:128 * (ci + 1), c0:c0 + TOKCH]))
                if ci >= 1:
                    W(q, "wvV", 4 * ci)
                    b.dinc("wvS" + z2, cc * NCI + ci - 1, q.dma_start(
                        out=wspill[128 * (ci - 1):128 * ci, c0:c0 + TOKCH],
                        in_=wtile[(ci - 1) % 2][:, 0:TOKCH]))
        b.Q(q_wv)

        def v_wv(v, z=z):
            b.w(v, "l2V")
            for ci in range(NCI):
                b.dwait(v, "wvL" + z2, cc * NCI + ci)
                b.w(v, "wvV")
                b.inc("wvV", v.tensor_sub(out=vf32[:, 0:TOKCH],
                                          in0=st_in[ci % 2][:, 0:TOKCH],
                                          in1=mu_bc[:, 0:TOKCH]))
                b.w(v, "wvV")
                b.inc("wvV", v.tensor_mul(out=vf32[:, 0:TOKCH], in0=vf32[:, 0:TOKCH],
                                          in1=r_bc[:, 0:TOKCH]))
                b.w(v, "wvV")
                if ci >= 2:
                    b.dwait(v, "wvS" + z2, cc * NCI + ci - 2)
                b.inc("wvV", v.tensor_add(out=wtile[ci % 2][:, 0:TOKCH],
                                          in0=st_in[ci % 2][:, 0:TOKCH],
                                          in1=vf32[:, 0:TOKCH]))
                b.inc("wvV", v.tensor_copy(out=vb[ci][:, 0:TOKCH],
                                           in_=vf32[:, 0:TOKCH]))
        b.V(v_wv)

        def s_l3(s, z=z):
            for ci in range(NCI):
                W(s, "wvV", 4 * (ci + 1))
                if ci >= 2:
                    W(s, "l3P", ci - 1)
                b.inc("l3q", s.activation(out=st_sq[ci % 2][:, 0:TOKCH],
                                              in_=vb[ci][:, 0:TOKCH], func=AF.Square))
        b.S(s_l3)

        stats_from(b, "l3", "l3q", 1,
                   lambda ci: vb[ci], lambda ci: st_sq[ci % 2],
                   b.cnt["l3q"] - NCI)

        def v_xt(v, cc=cc, z=z):
            b.w(v, "l3V")
            for ci in range(NCI):
                b.w(v, "xtV")
                b.inc("xtV", v.tensor_sub(out=vf32[:, 0:TOKCH],
                                              in0=vb[ci][:, 0:TOKCH],
                                              in1=mu_bc[:, 0:TOKCH]))
                b.w(v, "xtV")
                b.inc("xtV", v.tensor_mul(out=vf32[:, 0:TOKCH],
                                              in0=vf32[:, 0:TOKCH],
                                              in1=r_bc[:, 0:TOKCH]))
                b.w(v, "xtV")
                b.inc("xtV", v.tensor_scalar(out=xt[ci][:, 0:TOKCH],
                                                 in0=vf32[:, 0:TOKCH],
                                                 scalar1=lnw[:, ci:ci + 1],
                                                 scalar2=lnb[:, ci:ci + 1],
                                                 op0=OP.mult, op1=OP.add))
            if cc == 0:
                b.w(v, "MKV", 1)
                for ci in range(NCI):
                    b.w(v, "xtV")
                    b.inc("xtV", v.tensor_tensor(out=xt[ci][:, 0:HALO],
                                                     in0=xt[ci][:, 0:HALO],
                                                     in1=maskbc[:, 0:HALO], op=OP.mult))
            bnd_save = [bndA, bndB][(cc + 1) % 2]
            for ci in range(NCI):
                b.w(v, "xtV")
                b.inc("xtV", v.tensor_copy(out=bnd_save[:, ci:ci + 1],
                                               in_=xt[ci][:, TOKCH - 1:TOKCH]))
        b.V(v_xt)

        WSRC = [("k", wkb), ("v", wvb), ("r", wrb), ("o", wob)]

        def q_wp(q, cc=cc, z=z):
            for wi, (nm, src) in enumerate(WSRC):
                if wi >= 2:
                    W(q, "pm" + WSRC[wi - 2][0], 16)
                for ci in range(NCI):
                    b.inc("wp" + nm, q.dma_start(
                        out=wproj[wi % 2][:, 1024 * ci:1024 * (ci + 1)],
                        in_=src[128 * ci:128 * (ci + 1), :]), 16)
        b.Q(q_wp)

        bnd_prev = [bndA, bndB][cc % 2]
        NOFF = [(0, 512), (512, TOKCH - 512)]
        for wi, (pn, dst) in enumerate([("k", K), ("v", Vt), ("r", R)]):
            def v_mx(v, pn=pn, z=z):
                b.w(v, "xtV")
                if pn != "k":
                    W(v, "pm" + {"v": "k", "r": "v"}[pn], 16)
                for ci in range(NCI):
                    b.inc("mx" + pn, v.tensor_scalar(
                        out=mx[ci][:, 0:TOKCH], in0=xt[ci][:, 0:TOKCH],
                        scalar1=mixs[pn][:, ci:ci + 1], scalar2=None, op0=OP.mult))
                    b.w(v, "mx" + pn)
                    b.inc("mx" + pn, v.scalar_tensor_tensor(
                        out=mx[ci][:, 1:TOKCH], in0=xt[ci][:, 0:TOKCH - 1],
                        scalar=mix1m[pn][:, ci:ci + 1], in1=mx[ci][:, 1:TOKCH],
                        op0=OP.mult, op1=OP.add))
                    b.w(v, "mx" + pn)
                    b.inc("mx" + pn, v.scalar_tensor_tensor(
                        out=mx[ci][:, 0:1], in0=bnd_prev[:, ci:ci + 1],
                        scalar=mix1m[pn][:, ci:ci + 1], in1=mx[ci][:, 0:1],
                        op0=OP.mult, op1=OP.add))
            b.V(v_mx)

            def t_pj(t, pn=pn, wi=wi, z=z):
                W(t, "mx" + pn, 3 * NCI)
                W(t, "wp" + pn, 128)
                gi = 0
                for do in range(NCI):
                    for noff, ncw in NOFF:
                        ps = [PS_C, PS_D][gi % 2]
                        if gi >= 2:
                            W(t, "pe" + pn, gi - 1)
                        for ci in range(NCI):
                            ii = t.matmul(ps[:, 0:ncw],
                                          wproj[wi % 2][:, 1024 * ci + 128 * do:
                                                        1024 * ci + 128 * do + 128],
                                          mx[ci][:, noff:noff + ncw],
                                          start=(ci == 0), stop=(ci == NCI - 1),
                                          skip_group_check=True)
                        b.inc("pm" + pn, ii)
                        gi += 1
            b.T(t_pj)

            if pn == "r":
                def s_ev(s, pn=pn, dst=dst, z=z):
                    gi = 0
                    for do in range(NCI):
                        for noff, ncw in NOFF:
                            ps = [PS_C, PS_D][gi % 2]
                            W(s, "pm" + pn, gi + 1)
                            b.inc("pe" + pn, s.activation(
                                out=dst[do][:, noff:noff + ncw], in_=ps[:, 0:ncw],
                                func=AF.Sigmoid))
                            gi += 1
                b.S(s_ev)
            else:
                def v_ev(v, pn=pn, dst=dst, z=z):
                    gi = 0
                    for do in range(NCI):
                        for noff, ncw in NOFF:
                            ps = [PS_C, PS_D][gi % 2]
                            W(v, "pm" + pn, gi + 1)
                            b.inc("pe" + pn, v.tensor_copy(
                                out=dst[do][:, noff:noff + ncw], in_=ps[:, 0:ncw]))
                            gi += 1
                b.V(v_ev)

        if DBG and cc == 0:
            def g_dbg(g, z=z):
                W(g, "pek", 16)
                W(g, "pev", 16)
                W(g, "per", 16)
                for ci in range(NCI):
                    b.inc("DBG2", g.dma_start(out=o_dk[128 * ci:128 * (ci + 1), :],
                                              in_=K[ci][:, 0:TOKCH]), 16)
                    b.inc("DBG2", g.dma_start(out=o_dv[128 * ci:128 * (ci + 1), :],
                                              in_=Vt[ci][:, 0:TOKCH]), 16)
                    b.inc("DBG2", g.dma_start(out=o_dr[128 * ci:128 * (ci + 1), :],
                                              in_=R[ci][:, 0:TOKCH]), 16)
                    b.inc("DBG2", g.dma_start(out=o_dxt[128 * ci:128 * (ci + 1), :],
                                              in_=xt[ci][:, 0:TOKCH]), 16)
            b.G(g_dbg)

            def q_dbgw(q, c0=c0, z=z):
                for ci in range(NCI):
                    b.dwait(q, "wvS", ci)
                    b.inc("DBG1", q.dma_start(out=o_dw[128 * ci:128 * (ci + 1), c0:c0 + TOKCH],
                                              in_=wspill[128 * ci:128 * (ci + 1), c0:c0 + TOKCH]), 16)
            b.Q(q_dbgw)

        def v_scan(v, cc=cc, z=z):
            W(v, "pek", 16)
            W(v, "pev", 16)
            W(v, "per", 16)
            if DBG and cc == 0:
                b.w(v, "DBG2")
            b.w(v, "INI2", 1)
            for ci in range(NCI):
                ini = 0.0 if cc == 0 else carry[:, ci:ci + 1]
                b.inc("sc", v.tensor_tensor_scan(
                    out=Ssc[:, 0:TOKCH],
                    data0=dk[:, ci:ci + 1].to_broadcast([128, TOKCH]),
                    data1=K[ci][:, 0:TOKCH],
                    initial=ini, op0=OP.mult, op1=OP.add))
                b.w(v, "sc")
                b.inc("sc", v.tensor_copy(out=carry[:, ci:ci + 1],
                                              in_=Ssc[:, TOKCH - 1:TOKCH]))
                b.inc("sc", v.scalar_tensor_tensor(
                    out=K[ci][:, 0:TOKCH], in0=Ssc[:, 0:TOKCH],
                    scalar=tf[:, ci:ci + 1], in1=K[ci][:, 0:TOKCH],
                    op0=OP.mult, op1=OP.add))
                b.w(v, "sc")
                b.inc("sc", v.tensor_mul(out=Vt[ci][:, 0:TOKCH],
                                             in0=K[ci][:, 0:TOKCH],
                                             in1=Vt[ci][:, 0:TOKCH]))
                b.w(v, "sc")
                b.inc("sc", v.tensor_mul(out=R[ci][:, 0:TOKCH],
                                             in0=Vt[ci][:, 0:TOKCH],
                                             in1=R[ci][:, 0:TOKCH]))
        b.V(v_scan)

        def q_wl(q, c0=c0, cc=cc, z=z):
            lo, hi, ooff = CC_OWN[cc]
            for do in range(NCI + 1):
                if do < NCI:
                    if do >= 2:
                        W(q, "ye", 2 * (do - 1))
                    else:
                        b.dwait_all(q, "wvS" + z2, (cc + 1) * NCI)
                    b.dinc("wl" + z2, cc * NCI + do, q.dma_start(
                        out=wtile[do % 2][:, 0:TOKCH],
                        in_=wspill[128 * do:128 * (do + 1), c0:c0 + TOKCH]))
                if do >= 1:
                    W(q, "ye", 2 * do)
                    b.dinc("x2s" + z2, cc * NCI + do - 1, q.dma_start(
                        out=o_x2[128 * (do - 1):128 * do, ooff:ooff + (hi - lo)],
                        in_=x2t[(do - 1) % 2][:, lo:hi]))
        b.Q(q_wl)

        def t_y(t, z=z):
            W(t, "sc", 5 * NCI)
            W(t, "wpo", 128)
            gi = 0
            for do in range(NCI):
                for noff, ncw in NOFF:
                    ps = [PS_C, PS_D][gi % 2]
                    if gi >= 2:
                        W(t, "ye", gi - 1)
                    for ci in range(NCI):
                        ii = t.matmul(ps[:, 0:ncw],
                                      wproj[1][:, 1024 * ci + 128 * do:
                                                1024 * ci + 128 * do + 128],
                                      R[ci][:, noff:noff + ncw],
                                      start=(ci == 0), stop=(ci == NCI - 1),
                                      skip_group_check=True)
                    b.inc("ym", ii)
                    gi += 1
        b.T(t_y)

        def v_y(v, cc=cc, z=z):
            lo, hi, ooff = CC_OWN[cc]
            gi = 0
            for do in range(NCI):
                b.dwait(v, "wl" + z2, cc * NCI + do)
                if cc * NCI + do >= 2:
                    b.dwait(v, "x2s" + z2, cc * NCI + do - 2)
                for noff, ncw in NOFF:
                    ps = [PS_C, PS_D][gi % 2]
                    W(v, "ym", gi + 1)
                    b.inc("ye", v.tensor_tensor(
                        out=x2t[do % 2][:, noff:noff + ncw], in0=ps[:, 0:ncw],
                        in1=wtile[do % 2][:, noff:noff + ncw], op=OP.add))
                    gi += 1
        b.V(v_y)


    b.pop()

    if stage <= 3:
        def g_fin3(g):
            b.dwait_all(g, "x2s", NCH * NCI)
        b.G(g_fin3)
        return b

    xs2 = [b.alloc(TOWN, F32) for _ in range(NCI)]
    w9s = b.alloc(9 * NCI, F32)
    prob2 = b.alloc((TOWN // 128) * 8, F32)
    murow = b.alloc(TOWN, F32)
    rrow = b.alloc(TOWN, F32)
    mu_b4 = b.alloc(TOWN, F32)
    r_b4 = b.alloc(TOWN, F32)

    def q_l4(q):
        b.dwait_all(q, "x2s", NCH * NCI)
        for ci in range(NCI):
            b.inc("L4L", q.dma_start(out=xs2[ci][:], in_=o_x2[128 * ci:128 * (ci + 1), :]), 16)
        for ci in range(NCI):
            b.inc("L4L", q.dma_start(out=w9s[:, 9 * ci:9 * ci + 9],
                                     in_=w9[128 * ci:128 * (ci + 1), :]), 16)
    b.Q(q_l4)
    l4l = b.cnt["L4L"]

    def pre4(eng):
        b.w(eng, "L4L", l4l)
    for sec in (b.V, b.S, b.T):
        sec(pre4)

    emit_ln_gate(b, xs2, TOWN, idf, w9s, onescol, onerow, epscol, nhalfcol,
                 PS_A, PS_B, PS_E, o_xn2, prob2, mrow, murow, rrow, mu_b4, r_b4, "G4")

    def q_out4(q):
        b.w(q, "G4VEc")
        b.inc("ST4", q.dma_start(out=o_pr2, in_=prob2[:, :]), 16)
    b.Q(q_out4)

    def g_fin(g):
        if DBG:
            b.w(g, "DBG1")
            b.w(g, "DBG2")
        b.w(g, "ST4")
        b.dwait_all(g, "G4XNS", NCI)
        b.dwait_all(g, "x2s", NCH * NCI)
    b.G(g_fin)
    return b



def build_ffn2():
    b = B(176 * 1024)
    nc = b.nc
    ntj = TOWN // 128
    x2h = nc.dram_tensor("x2h", [C, TOWN], F32, kind="ExternalInput").ap()
    xn2h = nc.dram_tensor("xn2h", [C, TOWN], BF16, kind="ExternalInput").ap()
    pr2 = nc.dram_tensor("pr2", [128, ntj * 8], F32, kind="ExternalInput").ap()
    tau2 = nc.dram_tensor("tau2", [1, 8], F32, kind="ExternalInput").ap()
    ident = nc.dram_tensor("ident", [128, 128], F32, kind="ExternalInput").ap()
    ut = nc.dram_tensor("ut", [128, 128], F32, kind="ExternalInput").ap()
    iota8 = nc.dram_tensor("iota8", [1, 8], F32, kind="ExternalInput").ap()
    iotaB = nc.dram_tensor("iotaB", [1, 8], F32, kind="ExternalInput").ap()
    w1b = nc.dram_tensor("w1b", [8 * C, FF], BF16, kind="ExternalInput").ap()
    w2b = nc.dram_tensor("w2b", [8 * FF, C], BF16, kind="ExternalInput").ap()
    o_x3 = nc.dram_tensor("o_x3", [C, TOWN], F32, kind="ExternalOutput").ap()
    frame = nc.dram_tensor("frame", [SLOTS, C], BF16, kind="Internal").ap()
    retf = nc.dram_tensor("retf", [SLOTS, C], BF16, kind="Internal").ap()
    tmscr = nc.dram_tensor("tmscr", [TOWN, C], BF16, kind="Internal").ap()

    b.start()
    PS_A = b.ctx.enter_context(nc.psum_tensor("PS_A", [128, 512], F32))
    PS_B = b.ctx.enter_context(nc.psum_tensor("PS_B", [128, 512], F32))
    PS_C = b.ctx.enter_context(nc.psum_tensor("PS_C", [128, 512], F32))
    PS_D = b.ctx.enter_context(nc.psum_tensor("PS_D", [128, 512], F32))
    PS_E = b.ctx.enter_context(nc.psum_tensor("PS_E", [128, 1024], F32))
    PS_F = b.ctx.enter_context(nc.psum_tensor("PS_F", [128, 1024], F32))

    prob = b.alloc(ntj * 8, F32)
    gate = b.alloc(ntj, F32)
    idx = b.alloc(ntj, I32)
    trow = b.alloc(24, F32)
    taur = trow[:, 0:8]
    io8r = trow[:, 8:16]
    ioBr = trow[:, 16:24]
    onescol = b.alloc(1, F32)
    onerow = b.alloc(128, F32)
    idf = b.alloc(128, F32)
    utsb = b.alloc(128, F32)

    def q_init(q):
        b.inc("LD", q.dma_start(out=prob[:, :], in_=pr2), 16)
        b.inc("LD", q.dma_start(out=taur[0:1, 0:8], in_=tau2), 16)
        b.inc("LD", q.dma_start(out=io8r[0:1, 0:8], in_=iota8), 16)
        b.inc("LD", q.dma_start(out=ioBr[0:1, 0:8], in_=iotaB), 16)
        b.inc("LD", q.dma_start(out=idf[:], in_=ident), 16)
        b.inc("LD", q.dma_start(out=utsb[:], in_=ut), 16)
    b.Q(q_init)
    ld0 = b.cnt["LD"]

    def v_init(v):
        b.w(v, "LD", ld0)
        v.memset(onescol[:, 0:1], 1.0)
        b.inc("INI", v.memset(onerow[0:1, :], 1.0))
    b.V(v_init)

    def pre_r(eng):
        b.w(eng, "LD", ld0)
        b.w(eng, "INI", 1)
    for sec in (b.V, b.S, b.T, b.G, b.Q):
        sec(pre_r)

    emit_routing(b, ntj, prob, trow, io8r, ioBr, onescol, onerow, utsb,
                 PS_C, PS_D, PS_A, gate, idx, "r")
    b.push()
    emit_ffn(b, ntj, xn2h, idx, w1b, w2b, frame, retf,
             idf, PS_A, PS_B, PS_E, PS_F, "f")
    b.pop()

    gtm = [b.ctx.enter_context(b.nc.sbuf_tensor("gtm%d" % i, [128, 1024], BF16)).ap()
           for i in range(2)]
    xht = [b.alloc(TOWN, F32) for _ in range(2)]
    ffnt = [b.alloc(TOWN, BF16) for _ in range(2)]
    x3t = [b.alloc(TOWN, F32) for _ in range(2)]

    gts = [b.alloc(1024, BF16) for _ in range(2)]

    def g_gb(g):
        b.w(g, "fRSTA")
        b.w(g, "fRSTB")
        for tj in range(ntj):
            if tj >= 2:
                b.w(g, "GBV", tj - 1)
            b.dinc("GBK", tj, g.indirect_dma_start(
                out=gtm[tj % 2][:], out_offset=None,
                in_=retf, in_offset=bass.IndirectOffsetOnAxis(idx[:, tj:tj + 1], 0),
                bounds_check=SLOTS - 1, oob_is_err=False))
    b.G(g_gb)

    def v_gb(v):
        for tj in range(ntj):
            b.dwait(v, "GBK", tj)
            if tj >= 2:
                b.dwait(v, "GBS", tj - 2)
            b.inc("GBV", v.tensor_scalar(out=gts[tj % 2][:], in0=gtm[tj % 2][:],
                                         scalar1=gate[:, tj:tj + 1], scalar2=None,
                                         op0=OP.mult))
    b.V(v_gb)

    def q_gb2(q):
        for tj in range(ntj):
            b.w(q, "GBV", tj + 1)
            b.dinc("GBS", tj, q.dma_start(out=tmscr[128 * tj:128 * (tj + 1), :],
                                          in_=gts[tj % 2][:]))
    b.Q(q_gb2)

    def q_x3(q):
        b.dwait_all(q, "GBS", ntj)
        for ci in range(NCI + 1):
            if ci < NCI:
                if ci >= 2:
                    b.w(q, "X3V", ci - 1)
                q.dma_start(out=ffnt[ci % 2][:],
                            in_=tmscr[:, 128 * ci:128 * (ci + 1)],
                            transpose=True).then_inc(b.sem("X3T" + "AB"[ci % 2]), 16)
                b.dinc("X3T", ci, q.dma_start(out=xht[ci % 2][:],
                                              in_=x2h[128 * ci:128 * (ci + 1), :]))
                b.cnt["X3T" + "AB"[ci % 2]] += 16
            if ci >= 1:
                b.w(q, "X3V", ci)
                b.dinc("X3S", ci - 1, q.dma_start(
                    out=o_x3[128 * (ci - 1):128 * ci, :], in_=x3t[(ci - 1) % 2][:]))
    b.Q(q_x3)

    def v_x3(v):
        for ci in range(NCI):
            b.dwait(v, "X3T", ci, per=32)
            if ci >= 2:
                b.dwait(v, "X3S", ci - 2)
            b.inc("X3V", v.tensor_tensor(out=x3t[ci % 2][:], in0=xht[ci % 2][:],
                                         in1=ffnt[ci % 2][:], op=OP.add))
    b.V(v_x3)

    def g_fin(g):
        b.dwait_all(g, "X3S", NCI)
    b.G(g_fin)
    return b



_PROGS = {}
DEBUG = {}


def _progs():
    if "b1" not in _PROGS:
        _PROGS["b1"] = build_gate1()
        _PROGS["b2c"] = build_tau(cast=True)
        _PROGS["b2"] = build_tau()
        _PROGS["b3"] = build_main()
        _PROGS["b5"] = build_ffn2()
    return _PROGS


def kernel(L0_attn_x, L0_attn_time_mix_k, L0_attn_time_mix_v, L0_attn_time_mix_r,
           L0_attn_time_decay, L0_attn_time_first, L0_attn_Wk, L0_attn_Wv,
           L0_attn_Wr, L0_attn_Wo, L0_attn_ln_w, L0_attn_ln_b, L0_ffn_Wg,
           L0_ffn_W1, L0_ffn_W2):
    P = _progs()
    cores = list(range(NCORES))
    f32 = np.float32
    x = np.asarray(L0_attn_x, f32)[0]
    xT = np.ascontiguousarray(x.T)
    Wg = np.asarray(L0_ffn_Wg, f32)
    W1 = np.asarray(L0_ffn_W1, f32)
    W2 = np.asarray(L0_ffn_W2, f32)
    w9 = np.ascontiguousarray(np.concatenate([Wg, np.ones((C, 1), f32)], 1))
    ident = np.eye(128, dtype=f32)
    ut = np.triu(np.ones((128, 128), f32), 1)
    iota8 = np.arange(8, dtype=f32).reshape(1, 8)
    iotaB = (np.arange(8, dtype=f32) + 1024.0).reshape(1, 8)
    grid0 = np.arange(1, TAU_W + 1, dtype=f32).reshape(1, TAU_W)
    attnW = [np.ascontiguousarray(np.asarray(w, f32).T)
             for w in (L0_attn_Wk, L0_attn_Wv, L0_attn_Wr, L0_attn_Wo)]

    ins1 = []
    wcs = []
    for c in cores:
        lo = TOWN * c - HALO
        xh = np.zeros((C, TT), f32)
        s0 = max(lo, 0)
        xh[:, s0 - lo:] = xT[:, s0:TOWN * (c + 1)]
        wcs.append(np.ascontiguousarray(np.concatenate(
            [W1[c], W2[c].reshape(C, FF), attnW[c % 4].reshape(512, FF)], 0)))
        ins1.append({"xh": xh, "w9": w9, "ident": ident})
    r1 = run_bass_kernel_spmd(P["b1"].nc, ins1, core_ids=cores).results
    probs_full = np.concatenate(
        [r1[c]["o_pr"].reshape(128, NTJ, 8).transpose(1, 0, 2).reshape(TT, 8)[HALO:]
         for c in cores], 0)
    DEBUG["probs1"] = probs_full

    ins2 = [{"pp": np.ascontiguousarray(probs_full[:, [c] + [e for e in range(8) if e != c]]),
             "grid0": grid0, "wc_in": wcs[c]} for c in cores]
    r2 = run_bass_kernel_spmd(P["b2c"].nc, ins2, core_ids=cores).results
    tau1 = np.array([[float(r2[c]["o_tau"][0, 0]) for c in cores]], f32)
    w1b = np.concatenate([r2[c]["o_wc"][0:C] for c in cores], 0)
    w2b = np.concatenate([r2[c]["o_wc"][C:2 * C].reshape(FF, C) for c in cores], 0)
    wkb, wvb, wrb, wob = (r2[c]["o_wc"][2 * C:].reshape(C, C) for c in range(4))
    DEBUG["tau1"] = tau1

    def cols(vec):
        return np.ascontiguousarray(np.asarray(vec, f32).reshape(8, 128).T)
    mixk, mixv, mixr = cols(L0_attn_time_mix_k), cols(L0_attn_time_mix_v), cols(L0_attn_time_mix_r)
    tdk, tfk = cols(L0_attn_time_decay), cols(L0_attn_time_first)
    lnwk, lnbk = cols(L0_attn_ln_w), cols(L0_attn_ln_b)
    ins3 = []
    for c in cores:
        hm = np.ones((1, HALO), f32)
        if c == 0:
            hm[:] = 0.0
        ins3.append({
            "xh": ins1[c]["xh"], "xn1h": r1[c]["o_xn"], "pr1": r1[c]["o_pr"],
            "tau1": tau1, "w9": w9, "ident": ident, "ut": ut,
            "iota8": iota8, "iotaB": iotaB, "hmask": hm,
            "mixk": mixk, "mixv": mixv, "mixr": mixr, "tdk": tdk, "tfk": tfk,
            "lnwk": lnwk, "lnbk": lnbk, "w1b": w1b, "w2b": w2b,
            "wkb": wkb, "wvb": wvb, "wrb": wrb, "wob": wob,
        })
    r3 = run_bass_kernel_spmd(P["b3"].nc, ins3, core_ids=cores).results
    for dk_ in ("o_dx1", "o_dw", "o_dk", "o_dv", "o_dr", "o_dxt", "o_didx", "o_dgate", "o_rtaub", "o_riotab", "o_riotBb", "o_rt2e", "o_rasn", "o_rcrank", "o_roffb"):
        if dk_ in r3[0]:
            DEBUG[dk_] = [r3[c][dk_] for c in cores]
    probs2_full = np.concatenate(
        [r3[c]["o_pr2"].reshape(128, TOWN // 128, 8).transpose(1, 0, 2).reshape(TOWN, 8)
         for c in cores], 0)
    DEBUG["x2"] = np.concatenate([r3[c]["o_x2"].T for c in cores], 0)
    DEBUG["probs2"] = probs2_full

    ins4 = [{"pp": np.ascontiguousarray(probs2_full[:, [c] + [e for e in range(8) if e != c]]),
             "grid0": grid0} for c in cores]
    r4 = run_bass_kernel_spmd(P["b2"].nc, ins4, core_ids=cores).results
    tau2 = np.array([[float(r4[c]["o_tau"][0, 0]) for c in cores]], f32)
    DEBUG["tau2"] = tau2

    ins5 = []
    for c in cores:
        ins5.append({
            "x2h": r3[c]["o_x2"], "xn2h": r3[c]["o_xn2"], "pr2": r3[c]["o_pr2"],
            "tau2": tau2, "ident": ident, "ut": ut, "iota8": iota8, "iotaB": iotaB,
            "w1b": w1b, "w2b": w2b,
        })
    r5 = run_bass_kernel_spmd(P["b5"].nc, ins5, core_ids=cores).results
    out = np.concatenate([r5[c]["o_x3"].T for c in cores], 0)
    return np.ascontiguousarray(out).reshape(1, 16384, C)
